# version 01
# baseline (speedup 1.0000x reference)
import numpy as np
from contextlib import ExitStack

import concourse.bass as bass
import concourse.mybir as mybir
from concourse.bass_utils import run_bass_kernel_spmd

F32 = mybir.dt.float32
AF = mybir.ActivationFunctionType
ALU = mybir.AluOpType
AX = mybir.AxisListType

B, T, E = 4, 4096, 1024
PAGE = 256
TL = 2048
LN_EPS = 1e-5
RMS_EPS = 1e-6
SCALE = 0.125


class Counter:
    def __init__(self):
        self.n = {}

    def inc(self, inst, sem, k=1):
        inst.then_inc(sem, k)
        self.n[sem] = self.n.get(sem, 0) + k
        return self.n[sem]

    def val(self, sem):
        return self.n.get(sem, 0)


def sbap(t, pitch, p0, npar, off, dims):
    return bass.AP(t, p0 * pitch + off, [[pitch, npar]] + dims)


def build(upto=99, taps=None):
    taps = taps or {}
    nc = bass.Bass("TRN2", target_bir_lowering=False)
    C = Counter()

    def din(name, shape):
        return nc.dram_tensor(name, shape, F32, kind="ExternalInput")

    xmyT = din("xmyT", [E, TL])
    xseqT = din("xseqT", [E, T])
    wq_i = din("wq", [E, E])
    wdown_i = din("w_down", [E, 512])
    wkup_i = din("wk_up", [512, E])
    wvup_i = din("wv_up", [512, E])
    wo_i = din("wo", [E, E])
    WqT_i = din("WqT", [E, E])
    WkT_i = din("WkT", [E, E])
    WvT_i = din("WvT", [E, E])
    WoT_i = din("WoT", [E, E])
    bqv_i = din("bqv", [E, 4])
    nqk_i = din("nqk", [E, 2])
    rout_i = din("router_w", [E, 8])
    gateB = din("gateB", [8, 8, 8, 128, 128])
    upB = din("upB", [8, 8, 8, 128, 128])
    downB = din("downB", [8, 8, 8, 128, 128])
    sgateB = din("sgateB", [16, 8, 128, 128])
    supB = din("supB", [16, 8, 128, 128])
    sdownB = din("sdownB", [2, 8, 8, 128, 128])
    cdup_i = din("cdup_in", [128, T])
    sdup_i = din("sdup_in", [128, T])
    cdupQ_i = din("cdupQ", [128, TL])
    sdupQ_i = din("sdupQ", [128, TL])
    bones_i = din("bones", [128, 2])
    rankw_i = din("rankw", [8, 7])
    bvrow_i = din("bvrow_in", [1, 1024])
    outT = nc.dram_tensor("outT", [E, TL], F32, kind="ExternalOutput")

    kbuf = nc.dram_tensor("kbuf", [E, T], F32)
    vbuf = nc.dram_tensor("vbuf", [8, T, 128], F32)
    qbuf = nc.dram_tensor("qbuf", [E, TL], F32)
    CIN_K = 0
    CIN_V = E * TL
    CIN_M = CIN_V + 8 * TL * 128
    CIN_N = CIN_M + 128
    cin = nc.dram_tensor("cin", [CIN_N], F32)
    cout = nc.dram_tensor("cout", [2, CIN_N], F32)

    es = ExitStack()
    uid = [0]

    def _u(n):
        uid[0] += 1
        return f"{n}_u{uid[0]}"

    sb = lambda name, p, n: es.enter_context(nc.sbuf_tensor(name, [p, n], F32))
    xres = sb("xres", 128, 8 * TL)
    work = sb("work", 128, 16384)
    bones = sb("bonest", 128, 2)
    onecol = sb("onecol", 128, 1)
    onerow = sb("onerow", 1, 128)
    tiny = sb("tiny", 16, 2048)
    sq = sb("sqbuf", 128, 4096)
    hmax = sb("hmax", 2, 18)
    biasall = sb("biasall", 128, 16)
    rankw = sb("rankwt", 8, 7)
    bqvt = sb("bqvt", 128, 8 * 4)
    nqkt = sb("nqkt", 128, 8 * 2)
    routw = sb("routwt", 128, 8 * 8)

    s_ld = es.enter_context(nc.semaphore("s_ld"))
    s_st = es.enter_context(nc.semaphore("s_st"))
    s_t = es.enter_context(nc.semaphore("s_t"))
    s_v = es.enter_context(nc.semaphore("s_v"))
    s_a = es.enter_context(nc.semaphore("s_a"))
    s_g = es.enter_context(nc.semaphore("s_g"))
    s_x = es.enter_context(nc.semaphore("s_x"))

    phase = [0]

    def phase_on():
        phase[0] += 1
        return phase[0] <= upto

    def tap_dram(name, dram_t, shape):
        if name not in taps:
            return
        tout = nc.dram_tensor("tap_" + name, shape, F32, kind="ExternalOutput")
        n = 1
        for s in shape:
            n *= s
        with nc.Block() as blk:
            @blk.gpsimd
            def _(g):
                g.wait_ge(s_st, C.val(s_st))
                g.wait_ge(s_v, C.val(s_v))
                C.inc(g.dma_start(out=bass.AP(tout, 0, [[n, 1], [1, n]]),
                                  in_=bass.AP(dram_t, 0, [[n, 1], [1, n]])), s_st, 16)
                g.wait_ge(s_st, C.val(s_st))

    def tap(name, src_ap, rows, cols):
        if name not in taps:
            return
        tout = nc.dram_tensor("tap_" + name, [rows, cols], F32, kind="ExternalOutput")
        with nc.Block() as blk:
            @blk.gpsimd
            def _(g):
                g.wait_ge(s_t, C.val(s_t))
                g.wait_ge(s_v, C.val(s_v))
                g.wait_ge(s_a, C.val(s_a))
                C.inc(g.dma_start(out=tout.ap(), in_=src_ap), s_st, 16)
                g.wait_ge(s_st, C.val(s_st))

    if phase_on():
        with nc.Block() as blk:
            @blk.sync
            def _(s):
                C.inc(s.dma_start(out=bones[:, :], in_=bones_i.ap()), s_ld, 16)
                C.inc(s.dma_start(out=rankw[:, :], in_=rankw_i.ap()), s_ld, 16)
                C.inc(s.dma_start(
                    out=sbap(bqvt, 32, 0, 128, 0, [[4, 8], [1, 4]]),
                    in_=bass.AP(bqv_i, 0, [[4, 128], [128 * 4, 8], [1, 4]])), s_ld, 16)
                C.inc(s.dma_start(
                    out=sbap(nqkt, 16, 0, 128, 0, [[2, 8], [1, 2]]),
                    in_=bass.AP(nqk_i, 0, [[2, 128], [128 * 2, 8], [1, 2]])), s_ld, 16)
                C.inc(s.dma_start(
                    out=sbap(routw, 64, 0, 128, 0, [[8, 8], [1, 8]]),
                    in_=bass.AP(rout_i, 0, [[8, 128], [128 * 8, 8], [1, 8]])), s_ld, 16)

            @blk.vector
            def _(v):
                v.wait_ge(s_ld, C.val(s_ld))
                C.inc(v.memset(onecol[:, :], 1.0), s_v)
                C.inc(v.memset(onerow[:, :], 1.0), s_v)

    def ln_tile(blk, src_fn, dst_fn, ntok, st1, st2, bc1, bc2, eps=LN_EPS):
        m = tiny[0:1, 0:ntok]
        var = tiny[0:1, 512:512 + ntok]
        tmp = tiny[0:1, 1024:1024 + ntok]
        r = tiny[0:1, 1536:1536 + ntok]

        @blk.scalar
        def _(a):
            a.wait_ge(s_ld, C.val(s_ld))
            a.wait_ge(s_v, C.val(s_v))
            for c in range(8):
                i2 = a.activation(sq[:, c * ntok:(c + 1) * ntok], src_fn(c), AF.Square)
            C.inc(i2, s_a)

        @blk.tensor
        def _(t):
            t.wait_ge(s_a, C.val(s_a))
            t.wait_ge(s_v, C.val(s_v))
            for c in range(8):
                t.matmul(st1[0:1, 0:ntok], onecol[:, :], src_fn(c),
                         start=(c == 0), stop=(c == 7))
            for c in range(8):
                i2 = t.matmul(st2[0:1, 0:ntok], onecol[:, :],
                              sq[:, c * ntok:(c + 1) * ntok],
                              start=(c == 0), stop=(c == 7))
            C.inc(i2, s_t)

        @blk.vector
        def _(v):
            v.wait_ge(s_t, C.val(s_t))
            C.inc(v.tensor_scalar_mul(m, st1[0:1, 0:ntok], 1.0 / 1024.0), s_v)
            C.inc(v.tensor_tensor(tmp, m, m, ALU.mult), s_v)
            C.inc(v.scalar_tensor_tensor(var, st2[0:1, 0:ntok], 1.0 / 1024.0,
                                         tmp, ALU.mult, ALU.subtract), s_v)
            C.inc(v.tensor_scalar_add(var, var, eps), s_v)
            C.inc(v.reciprocal(tmp, var), s_v)

        @blk.scalar
        def _(a):
            a.wait_ge(s_v, C.val(s_v))
            C.inc(a.activation(r, tmp, AF.Sqrt), s_a)

        @blk.vector
        def _(v):
            v.wait_ge(s_a, C.val(s_a))
            for _i in range(2):
                C.inc(v.tensor_tensor(tmp, r, r, ALU.mult), s_v)
                C.inc(v.tensor_tensor(tmp, var, tmp, ALU.mult), s_v)
                C.inc(v.tensor_scalar(tmp, tmp, -0.5, 1.5, ALU.mult, ALU.add), s_v)
                C.inc(v.tensor_tensor(r, r, tmp, ALU.mult), s_v)
            C.inc(v.tensor_tensor(m, m, r, ALU.mult), s_v)

        @blk.tensor
        def _(t):
            t.wait_ge(s_v, C.val(s_v))
            t.matmul(bc1[:, 0:ntok], onerow[0:1, :], r, start=True, stop=True)
            C.inc(t.matmul(bc2[:, 0:ntok], onerow[0:1, :], m, start=True, stop=True), s_t)

        @blk.vector
        def _(v):
            v.wait_ge(s_t, C.val(s_t))
            for c in range(8):
                v.tensor_tensor(dst_fn(c), src_fn(c), bc1[:, 0:ntok], ALU.mult)
                i2 = v.tensor_tensor(dst_fn(c), dst_fn(c), bc2[:, 0:ntok], ALU.subtract)
            C.inc(i2, s_v)

    if phase_on():
        with (
            nc.sbuf_tensor(_u("wdt"), [128, 8 * 512], F32) as wdt,
            nc.sbuf_tensor(_u("xs1"), [128, 8 * 512], F32) as xs1,
            nc.psum_tensor(_u("pstat1"), [1, 512], F32) as pstat1,
            nc.psum_tensor(_u("pstat2"), [1, 512], F32) as pstat2,
            nc.psum_tensor(_u("pbc1"), [128, 512], F32) as pbc1,
            nc.psum_tensor(_u("pbc2"), [128, 512], F32) as pbc2,
            nc.psum_tensor(_u("plat"), [128, 512], F32) as plat,
        ):
            with nc.Block() as blk:
                @blk.sync
                def _(s):
                    C.inc(s.dma_start(
                        out=sbap(wdt, 8 * 512, 0, 128, 0, [[512, 8], [1, 512]]),
                        in_=bass.AP(wdown_i, 0, [[512, 128], [128 * 512, 8], [1, 512]])),
                        s_ld, 16)
            for tt in range(8):
                with nc.Block() as blk:
                    @blk.sync
                    def _(s, tt=tt):
                        s.wait_ge(s_t, C.val(s_t))
                        s.wait_ge(s_v, C.val(s_v))
                        s.wait_ge(s_a, C.val(s_a))
                        C.inc(s.dma_start(
                            out=sbap(xs1, 8 * 512, 0, 128, 0, [[512, 8], [1, 512]]),
                            in_=bass.AP(xseqT, tt * 512, [[T, 128], [128 * T, 8], [1, 512]])),
                            s_ld, 16)
                    ln_tile(blk,
                            lambda c: xs1[:, c * 512:(c + 1) * 512],
                            lambda c: xs1[:, c * 512:(c + 1) * 512],
                            512, pstat1, pstat2, pbc1, pbc2)
                    for oc in range(4):
                        @blk.tensor
                        def _(t, oc=oc):
                            t.wait_ge(s_v, C.val(s_v))
                            for ic in range(8):
                                i2 = t.matmul(plat[:, :],
                                              wdt[:, ic * 512 + oc * 128: ic * 512 + oc * 128 + 128],
                                              xs1[:, ic * 512:(ic + 1) * 512],
                                              start=(ic == 0), stop=(ic == 7))
                            C.inc(i2, s_t)

                        @blk.vector
                        def _(v, oc=oc, tt=tt):
                            v.wait_ge(s_t, C.val(s_t))
                            C.inc(v.tensor_copy(
                                work[:, oc * 4096 + tt * 512: oc * 4096 + tt * 512 + 512],
                                plat[:, :]), s_v)
        tap("lat", work[:, 0:16384], 128, 16384)

    if phase_on():
        with (
            nc.sbuf_tensor(_u("wkv"), [128, 4096], F32) as wkv,
            nc.sbuf_tensor(_u("cdup"), [128, 512], F32) as cdup,
            nc.sbuf_tensor(_u("sdup"), [128, 512], F32) as sdup,
            nc.psum_tensor(_u("pk0"), [128, 512], F32) as pk0,
            nc.psum_tensor(_u("pn"), [2, 512], F32) as pn,
        ):
            kv1 = sq[:, 0:512]
            kv2 = sq[:, 512:1024]
            kv3 = sq[:, 1024:1536]
            with nc.Block() as blk:
                @blk.sync
                def _(s):
                    C.inc(s.dma_start(
                        out=sbap(wkv, 4096, 0, 128, 0, [[1024, 4], [1, 1024]]),
                        in_=bass.AP(wkup_i, 0, [[1024, 128], [128 * 1024, 4], [1, 1024]])),
                        s_ld, 16)


                @blk.vector
                def _(v):
                    C.inc(v.memset(hmax[:, :], 0.0), s_v)

            for oc in range(8):
                for tt in range(8):
                    with nc.Block() as blk:
                        @blk.sync
                        def _(s, tt=tt):
                            s.wait_ge(s_v, C.val(s_v))
                            C.inc(s.dma_start(out=cdup[:, :],
                                              in_=cdup_i.ap()[:, tt * 512:(tt + 1) * 512]), s_ld, 16)
                            C.inc(s.dma_start(out=sdup[:, :],
                                              in_=sdup_i.ap()[:, tt * 512:(tt + 1) * 512]), s_ld, 16)

                        @blk.tensor
                        def _(t, oc=oc, tt=tt):
                            t.wait_ge(s_ld, C.val(s_ld))
                            t.wait_ge(s_st, C.val(s_st))
                            for ic in range(4):
                                i2 = t.matmul(pk0[:, :],
                                              wkv[:, ic * 1024 + oc * 128: ic * 1024 + oc * 128 + 128],
                                              work[:, ic * 4096 + tt * 512: ic * 4096 + tt * 512 + 512],
                                              start=(ic == 0), stop=(ic == 3))
                            C.inc(i2, s_t)

                        @blk.vector
                        def _(v, tt=tt):
                            v.wait_ge(s_t, C.val(s_t))
                            C.inc(v.tensor_copy(kv2, pk0[:, :]), s_v)
                            C.inc(v.stream_shuffle(kv3, kv2,
                                                   [j ^ 1 for j in range(32)]), s_v)
                            C.inc(v.tensor_tensor(kv1, kv2,
                                                  cdup[:, :], ALU.mult), s_v)
                            C.inc(v.tensor_tensor(kv3, kv3,
                                                  sdup[:, :], ALU.mult), s_v)
                            C.inc(v.tensor_add(kv1, kv1, kv3), s_v)

                        @blk.scalar
                        def _(a):
                            a.wait_ge(s_v, C.val(s_v))
                            C.inc(a.activation(kv2, kv1, AF.Square), s_a)

                        @blk.tensor
                        def _(t):
                            t.wait_ge(s_a, C.val(s_a))
                            C.inc(t.matmul(pn[:, :], bones[:, :], kv2,
                                           start=True, stop=True), s_t)

                        @blk.vector
                        def _(v, oc=oc):
                            v.wait_ge(s_t, C.val(s_t))
                            C.inc(v.reduce_max(hmax[0:2, 16:17], pn[0:2, :], axis=AX.X), s_v)
                            C.inc(v.tensor_max(hmax[0:2, oc:oc + 1], hmax[0:2, oc:oc + 1],
                                               hmax[0:2, 16:17]), s_v)

                        @blk.gpsimd
                        def _(g, oc=oc, tt=tt):
                            g.wait_ge(s_v, C.val(s_v))
                            C.inc(g.dma_start(
                                out=bass.AP(kbuf, oc * 128 * T + tt * 512, [[T, 128], [1, 512]]),
                                in_=kv1), s_st, 16)

            with nc.Block() as blk:
                @blk.sync
                def _(s):
                    s.wait_ge(s_t, C.val(s_t))
                    C.inc(s.dma_start(
                        out=sbap(wkv, 4096, 0, 128, 0, [[1024, 4], [1, 1024]]),
                        in_=bass.AP(wvup_i, 0, [[1024, 128], [128 * 1024, 4], [1, 1024]])),
                        s_ld, 16)
            for tt in range(32):
                for nt in range(2):
                    with nc.Block() as blk:
                        @blk.tensor
                        def _(t, tt=tt, nt=nt):
                            t.wait_ge(s_ld, C.val(s_ld))
                            t.wait_ge(s_st, C.val(s_st))
                            for ic in range(4):
                                i2 = t.matmul(pk0[:, :],
                                              work[:, ic * 4096 + tt * 128: ic * 4096 + tt * 128 + 128],
                                              wkv[:, ic * 1024 + nt * 512: ic * 1024 + nt * 512 + 512],
                                              start=(ic == 0), stop=(ic == 3))
                            C.inc(i2, s_t)

                        @blk.vector
                        def _(v):
                            v.wait_ge(s_t, C.val(s_t))
                            C.inc(v.tensor_copy(kv1, pk0[:, :]), s_v)

                        @blk.gpsimd
                        def _(g, tt=tt, nt=nt):
                            g.wait_ge(s_v, C.val(s_v))
                            for i in range(4):
                                hp = nt * 4 + i
                                C.inc(g.dma_start(
                                    out=bass.AP(vbuf, hp * T * 128 + tt * 128 * 128,
                                                [[128, 128], [1, 128]]),
                                    in_=kv1[:, i * 128:(i + 1) * 128]), s_st, 16)
        tap_dram("kbuf", kbuf, [E, T])
        tap_dram("vbuf", vbuf, [8 * T, 128])

    if phase_on():
        with (
            nc.sbuf_tensor(_u("cdq"), [128, TL], F32) as cdq,
            nc.sbuf_tensor(_u("sdq"), [128, TL], F32) as sdq,
            nc.psum_tensor(_u("pstat1"), [1, 512], F32) as pstat1,
            nc.psum_tensor(_u("pstat2"), [1, 512], F32) as pstat2,
            nc.psum_tensor(_u("pbc1"), [128, 512], F32) as pbc1,
            nc.psum_tensor(_u("pbc2"), [128, 512], F32) as pbc2,
            nc.psum_tensor(_u("pq0"), [128, 512], F32) as pq0,
            nc.psum_tensor(_u("pn"), [2, 512], F32) as pn,
        ):
            xs1 = work[:, 0:4096]
            wqt = work[:, 4096:12288]
            kv1 = work[:, 12288:12800]
            kv2 = work[:, 12800:13312]
            kv3 = work[:, 13312:13824]
            with nc.Block() as blk:
                @blk.sync
                def _(s):
                    s.wait_ge(s_t, C.val(s_t))
                    C.inc(s.dma_start(
                        out=bass.AP(work, 4096, [[16384, 128], [1024, 8], [1, 1024]]),
                        in_=bass.AP(wq_i, 0, [[1024, 128], [128 * 1024, 8], [1, 1024]])), s_ld, 16)
                    C.inc(s.dma_start(out=cdq[:, :], in_=cdupQ_i.ap()), s_ld, 16)
                    C.inc(s.dma_start(out=sdq[:, :], in_=sdupQ_i.ap()), s_ld, 16)
            for tt in range(4):
                with nc.Block() as blk:
                    @blk.sync
                    def _(s, tt=tt):
                        s.wait_ge(s_t, C.val(s_t))
                        s.wait_ge(s_v, C.val(s_v))
                        s.wait_ge(s_a, C.val(s_a))
                        C.inc(s.dma_start(
                            out=bass.AP(work, 0, [[16384, 128], [512, 8], [1, 512]]),
                            in_=bass.AP(xmyT, tt * 512, [[TL, 128], [128 * TL, 8], [1, 512]])),
                            s_ld, 16)
                    ln_tile(blk,
                            lambda c: work[:, c * 512:(c + 1) * 512],
                            lambda c: work[:, c * 512:(c + 1) * 512],
                            512, pstat1, pstat2, pbc1, pbc2)
                    for oc in range(8):
                        @blk.tensor
                        def _(t, oc=oc):
                            t.wait_ge(s_v, C.val(s_v))
                            t.wait_ge(s_st, C.val(s_st))
                            for ic in range(8):
                                i2 = t.matmul(pq0[:, :],
                                              work[:, 4096 + ic * 1024 + oc * 128:
                                                   4096 + ic * 1024 + oc * 128 + 128],
                                              work[:, ic * 512:(ic + 1) * 512],
                                              start=(ic == 0), stop=(ic == 7))
                            C.inc(i2, s_t)

                        @blk.vector
                        def _(v, tt=tt):
                            v.wait_ge(s_t, C.val(s_t))
                            C.inc(v.tensor_copy(kv2, pq0[:, :]), s_v)
                            C.inc(v.stream_shuffle(kv3, kv2,
                                                   [j ^ 1 for j in range(32)]), s_v)
                            C.inc(v.tensor_tensor(kv1, kv2,
                                                  cdq[:, tt * 512:(tt + 1) * 512], ALU.mult), s_v)
                            C.inc(v.tensor_tensor(kv3, kv3,
                                                  sdq[:, tt * 512:(tt + 1) * 512], ALU.mult), s_v)
                            C.inc(v.tensor_add(kv1, kv1, kv3), s_v)

                        @blk.scalar
                        def _(a):
                            a.wait_ge(s_v, C.val(s_v))
                            C.inc(a.activation(kv2, kv1, AF.Square), s_a)

                        @blk.tensor
                        def _(t):
                            t.wait_ge(s_a, C.val(s_a))
                            C.inc(t.matmul(pn[:, :], bones[:, :], kv2,
                                           start=True, stop=True), s_t)

                        @blk.vector
                        def _(v, oc=oc):
                            v.wait_ge(s_t, C.val(s_t))
                            C.inc(v.reduce_max(hmax[0:2, 16:17], pn[0:2, :], axis=AX.X), s_v)
                            C.inc(v.tensor_max(hmax[0:2, 8 + oc:9 + oc], hmax[0:2, 8 + oc:9 + oc],
                                               hmax[0:2, 16:17]), s_v)

                        @blk.gpsimd
                        def _(g, oc=oc, tt=tt):
                            g.wait_ge(s_v, C.val(s_v))
                            C.inc(g.dma_start(
                                out=bass.AP(qbuf, oc * 128 * TL + tt * 512, [[TL, 128], [1, 512]]),
                                in_=kv1), s_st, 16)

    if True:
        tap_dram("qbuf", qbuf, [E, TL])

    def bias_assembly():
        with (
            nc.sbuf_tensor(_u("mrow"), [2, 8], F32) as mrow,
            nc.psum_tensor(_u("pbias"), [128, 8], F32) as pbias,
            nc.Block() as blk,
        ):
            @blk.vector
            def _(v):
                v.wait_ge(s_v, C.val(s_v))
                i2 = None
                for oc in range(8):
                    i2 = v.tensor_tensor(mrow[0:2, oc:oc + 1], hmax[0:2, oc:oc + 1],
                                         hmax[0:2, 8 + oc:9 + oc], ALU.mult)
                C.inc(i2, s_v)

            @blk.scalar
            def _(a):
                a.wait_ge(s_v, C.val(s_v))
                C.inc(a.activation(mrow[0:2, 0:8], mrow[0:2, 0:8], AF.Sqrt), s_a)

            @blk.vector
            def _(v):
                v.wait_ge(s_a, C.val(s_a))
                C.inc(v.tensor_scalar_mul(mrow[0:2, 0:8], mrow[0:2, 0:8],
                                          -SCALE * 1.02), s_v)
                C.inc(v.stream_shuffle(sq[0:2, 0:8], mrow[0:2, 0:8],
                                       [1, 0] + list(range(2, 32))), s_v)

            @blk.tensor
            def _(t):
                t.wait_ge(s_v, C.val(s_v))
                C.inc(t.matmul(pbias[:, :], onerow[0:1, :], mrow[0:1, 0:8],
                               start=True, stop=True), s_t)

            @blk.vector
            def _(v):
                v.wait_ge(s_t, C.val(s_t))
                C.inc(v.tensor_copy(sbap(biasall, 16, 0, 128, 0, [[2, 8]]),
                                    pbias[:, :]), s_v)

            @blk.tensor
            def _(t):
                t.wait_ge(s_v, C.val(s_v))
                i2 = t.matmul(pbias[:, :], onerow[0:1, :], sq[0:1, 0:8], start=True, stop=True)
                C.inc(i2, s_t)

            @blk.vector
            def _(v):
                v.wait_ge(s_t, C.val(s_t))
                C.inc(v.tensor_copy(sbap(biasall, 16, 0, 128, 1, [[2, 8]]),
                                    pbias[:, :]), s_v)

    if phase_on():
        bias_assembly()
        tap("biasall", biasall[:, :], 128, 16)

    def emit_attention(causal):
        qtile = sq[:, 2048:4096]
        with (
            nc.sbuf_tensor(_u("ktile"), [128, 4096], F32) as ktile,
            nc.sbuf_tensor(_u("vtile"), [128, 32 * 193], F32) as vtile,

            nc.psum_tensor(_u("psc0"), [128, 512], F32) as psc0,
            nc.psum_tensor(_u("psc1"), [128, 512], F32) as psc1,
            nc.psum_tensor(_u("poe"), [128, 512], F32) as poe,
            nc.psum_tensor(_u("poo"), [128, 512], F32) as poo,
            nc.psum_tensor(_u("prc"), [128, 512], F32) as prc,
        ):
            for hp in range(8):
                with nc.Block() as blk:
                    @blk.sync
                    def _(s, hp=hp):
                        s.wait_ge(s_t, C.val(s_t))
                        s.wait_ge(s_st, C.val(s_st))
                        s.wait_ge(s_g, C.val(s_g))
                        if causal:
                            C.inc(s.dma_start(
                                out=ktile[:, :],
                                in_=bass.AP(kbuf, hp * 128 * T, [[T, 128], [1, T]])), s_ld, 16)
                            C.inc(s.dma_start(
                                out=sbap(vtile, 32 * 193, 0, 128, 0, [[193, 32], [1, 64]]),
                                in_=bass.AP(vbuf, hp * T * 128,
                                            [[128, 128], [128 * 128, 32], [1, 64]])), s_ld, 16)
                            C.inc(s.dma_start(
                                out=sbap(vtile, 32 * 193, 0, 128, 129, [[193, 32], [1, 64]]),
                                in_=bass.AP(vbuf, hp * T * 128 + 64,
                                            [[128, 128], [128 * 128, 32], [1, 64]])), s_ld, 16)
                        else:
                            for rk in range(2):
                                C.inc(s.dma_start(
                                    out=ktile[:, rk * TL:(rk + 1) * TL],
                                    in_=bass.AP(cout, rk * CIN_N + CIN_K + hp * 128 * TL,
                                                [[TL, 128], [1, TL]])), s_ld, 16)
                                C.inc(s.dma_start(
                                    out=sbap(vtile, 32 * 193, 0, 128, rk * 16 * 193,
                                             [[193, 16], [1, 64]]),
                                    in_=bass.AP(cout, rk * CIN_N + CIN_V + hp * TL * 128,
                                                [[128, 128], [128 * 128, 16], [1, 64]])), s_ld, 16)
                                C.inc(s.dma_start(
                                    out=sbap(vtile, 32 * 193, 0, 128, rk * 16 * 193 + 129,
                                             [[193, 16], [1, 64]]),
                                    in_=bass.AP(cout, rk * CIN_N + CIN_V + hp * TL * 128 + 64,
                                                [[128, 128], [128 * 128, 16], [1, 64]])), s_ld, 16)
                        C.inc(s.dma_start(
                            out=qtile[:, :],
                            in_=bass.AP(qbuf, hp * 128 * TL, [[TL, 128], [1, TL]])), s_ld, 16)

                    @blk.vector
                    def _(v):
                        v.wait_ge(s_ld, C.val(s_ld))
                        C.inc(v.memset(sbap(vtile, 32 * 193, 0, 128, 64, [[193, 32], [1, 1]]), 1.0), s_v)
                        C.inc(v.memset(sbap(vtile, 32 * 193, 0, 128, 65, [[193, 32], [1, 64]]), 0.0), s_v)
                        C.inc(v.memset(sbap(vtile, 32 * 193, 0, 128, 97, [[193, 32], [1, 1]]), 1.0), s_v)

                for hl in range(2):
                    h = 2 * hp + hl
                    for qt in range(4):
                        if causal:
                            chunks = [(kc, 0) for kc in range(8 * qt + 2)]
                            for r in range(1, 4):
                                chunks.append((8 * qt + 2 * r, 128 * r))
                                chunks.append((8 * qt + 2 * r + 1, 128 * r))
                        else:
                            chunks = [(kc, 0) for kc in range(32)]
                        nch = len(chunks)
                        with nc.Block() as blk:
                            t_base = C.val(s_t)
                            a_base = C.val(s_a)

                            @blk.tensor
                            def _(t, hp=hp, hl=hl, qt=qt, chunks=chunks,
                                  t_base=t_base, a_base=a_base):
                                t.wait_ge(s_ld, C.val(s_ld))
                                t.wait_ge(s_v, C.val(s_v))
                                po = poe if hl == 0 else poo
                                for i, (kc, qlo) in enumerate(chunks):
                                    psc = psc0 if (i % 2 == 0) else psc1
                                    nq = 512 - qlo
                                    if i >= 2:
                                        t.wait_ge(s_a, a_base + i - 1)
                                    C.inc(t.matmul(
                                        psc[0:128, 0:nq],
                                        ktile[hl * 64:hl * 64 + 64, kc * 128:(kc + 1) * 128],
                                        qtile[hl * 64:hl * 64 + 64,
                                              qt * 512 + qlo:(qt + 1) * 512],
                                        start=True, stop=True), s_t)
                                    if i >= 1:
                                        pkc, pqlo = chunks[i - 1]
                                        pnq = 512 - pqlo
                                        t.wait_ge(s_a, a_base + i)
                                        vsl = (vtile[:, pkc * 193:pkc * 193 + 65] if hl == 0
                                               else vtile[:, pkc * 193 + 65:pkc * 193 + 193])
                                        t.matmul(po[0:(65 if hl == 0 else 128), pqlo:512],
                                                 vsl,
                                                 sq[:, ((i - 1) % 2) * 512:
                                                    ((i - 1) % 2) * 512 + pnq],
                                                 start=(i == 1), stop=False)
                                lkc, lqlo = chunks[-1]
                                lnq = 512 - lqlo
                                t.wait_ge(s_a, a_base + nch)
                                vsl = (vtile[:, lkc * 193:lkc * 193 + 65] if hl == 0
                                       else vtile[:, lkc * 193 + 65:lkc * 193 + 193])
                                C.inc(t.matmul(po[0:(65 if hl == 0 else 128), lqlo:512],
                                               vsl,
                                               sq[:, ((nch - 1) % 2) * 512:
                                                  ((nch - 1) % 2) * 512 + lnq],
                                               start=(nch == 1), stop=True), s_x)

                            @blk.scalar
                            def _(a, h=h, chunks=chunks, t_base=t_base):
                                for i, (kc, qlo) in enumerate(chunks):
                                    nq = 512 - qlo
                                    psc = psc0 if (i % 2 == 0) else psc1
                                    a.wait_ge(s_t, t_base + i + 1)
                                    C.inc(a.activation(
                                        sq[:, (i % 2) * 512:(i % 2) * 512 + nq],
                                        psc[0:128, 0:nq], AF.Exp,
                                        bias=biasall[:, h:h + 1], scale=SCALE), s_a)

                            @blk.vector
                            def _(v, hl=hl):
                                v.wait_ge(s_x, C.val(s_x))
                                po = poe if hl == 0 else poo
                                dr = 64 if hl == 0 else 32
                                C.inc(v.reciprocal(sq[0:1, 1024:1536], po[dr:dr + 1, 0:512]), s_v)

                            @blk.tensor
                            def _(t, hl=hl):
                                t.wait_ge(s_v, C.val(s_v))
                                if hl == 0:
                                    i2 = t.matmul(prc[0:64, 0:512], onerow[0:1, 0:64],
                                                  sq[0:1, 1024:1536], start=True, stop=True)
                                else:
                                    i2 = t.matmul(prc[64:128, 0:512], onerow[0:1, 0:64],
                                                  sq[0:1, 1024:1536], start=True, stop=True)
                                C.inc(i2, s_t)

                            @blk.vector
                            def _(v, hl=hl, hp=hp, qt=qt):
                                v.wait_ge(s_t, C.val(s_t))
                                if hl == 0:
                                    C.inc(v.tensor_copy(sq[0:64, 1536:2048], prc[0:64, :]), s_v)
                                    C.inc(v.scalar_tensor_tensor(
                                        sbap(work, 16384, 0, 64, hp * 2048 + qt * 512, [[1, 512]]),
                                        poe[0:64, 0:512], 1.0, sq[0:64, 1536:2048],
                                        ALU.mult, ALU.mult), s_v)
                                else:
                                    C.inc(v.tensor_copy(sq[64:128, 1536:2048], prc[64:128, :]), s_v)
                                    C.inc(v.scalar_tensor_tensor(
                                        sbap(work, 16384, 64, 64, hp * 2048 + qt * 512, [[1, 512]]),
                                        poo[64:128, 0:512], 1.0, sq[64:128, 1536:2048],
                                        ALU.mult, ALU.mult), s_v)

    if phase_on():
        emit_attention(causal=True)
        tap("attnO", work[:, 0:16384], 128, 16384)

    def emit_out_proj(w_hbm, with_bias, load_x):
        with (
            nc.sbuf_tensor(_u("wot"), [128, 8192], F32) as wot,
            nc.psum_tensor(_u("pw0"), [128, 512], F32) as pw0,
        ):
            with nc.Block() as blk:
                @blk.sync
                def _(s):
                    C.inc(s.dma_start(
                        out=sbap(wot, 8192, 0, 128, 0, [[1024, 8], [1, 1024]]),
                        in_=bass.AP(w_hbm, 0, [[1024, 128], [128 * 1024, 8], [1, 1024]])), s_ld, 16)
                    if load_x:
                        C.inc(s.dma_start(
                            out=sbap(xres, 8 * TL, 0, 128, 0, [[TL, 8], [1, TL]]),
                            in_=bass.AP(xmyT, 0, [[TL, 128], [128 * TL, 8], [1, TL]])), s_ld, 16)
            for oc in range(8):
                for qt in range(4):
                    with nc.Block() as blk:
                        @blk.tensor
                        def _(t, oc=oc, qt=qt):
                            t.wait_ge(s_ld, C.val(s_ld))
                            t.wait_ge(s_v, C.val(s_v))
                            for ic in range(8):
                                i2 = t.matmul(pw0[:, :],
                                              wot[:, ic * 1024 + oc * 128: ic * 1024 + oc * 128 + 128],
                                              work[:, ic * 2048 + qt * 512: ic * 2048 + qt * 512 + 512],
                                              start=(ic == 0), stop=(ic == 7))
                            C.inc(i2, s_t)

                        @blk.vector
                        def _(v, oc=oc, qt=qt):
                            v.wait_ge(s_t, C.val(s_t))
                            dst = xres[:, oc * TL + qt * 512: oc * TL + qt * 512 + 512]
                            if with_bias:
                                C.inc(v.scalar_tensor_tensor(
                                    dst, pw0[:, :], bqvt[:, oc * 4 + 3:oc * 4 + 4], dst,
                                    ALU.add, ALU.add), s_v)
                            else:
                                C.inc(v.scalar_tensor_tensor(
                                    dst, pw0[:, :], 1.0, dst, ALU.mult, ALU.add), s_v)

    if phase_on():
        emit_out_proj(wo_i, with_bias=False, load_x=True)
        tap("x1", xres[:, 0:16384], 128, 16384)

    def emit_ffn(cdense):
        with (
            nc.psum_tensor(_u("pstat1"), [1, 512], F32) as pstat1,
            nc.psum_tensor(_u("pstat2"), [1, 512], F32) as pstat2,
            nc.psum_tensor(_u("pbc1"), [128, 512], F32) as pbc1,
            nc.psum_tensor(_u("pbc2"), [128, 512], F32) as pbc2,
        ):
            for tt in range(4):
                with nc.Block() as blk:
                    ln_tile(blk,
                            lambda c, tt=tt: xres[:, c * TL + tt * 512: c * TL + tt * 512 + 512],
                            lambda c, tt=tt: work[:, c * TL + tt * 512: c * TL + tt * 512 + 512],
                            512, pstat1, pstat2, pbc1, pbc2)

        with (
            nc.sbuf_tensor(_u("rk8"), [8, 3 * 2048], F32) as rk8,
            nc.psum_tensor(_u("prt"), [8, 512], F32) as prt,
            nc.psum_tensor(_u("pbc"), [8, 512], F32) as pbc,
        ):
            for tt in range(4):
                with nc.Block() as blk:
                    @blk.tensor
                    def _(t, tt=tt):
                        t.wait_ge(s_v, C.val(s_v))
                        for ic in range(8):
                            i2 = t.matmul(prt[:, :], routw[:, ic * 8:(ic + 1) * 8],
                                          work[:, ic * TL + tt * 512: ic * TL + tt * 512 + 512],
                                          start=(ic == 0), stop=(ic == 7))
                        C.inc(i2, s_t)

                    @blk.vector
                    def _(v, tt=tt):
                        v.wait_ge(s_t, C.val(s_t))
                        C.inc(v.tensor_copy(cdense[:, tt * 512:(tt + 1) * 512], prt[:, :]), s_v)

            P0 = rk8[0:8, 0:2048]
            P1 = rk8[0:8, 2048:4096]
            P2 = rk8[0:8, 4096:6144]
            with nc.Block() as blk:
                @blk.vector
                def _(v):
                    v.wait_ge(s_v, C.val(s_v))
                    C.inc(v.stream_shuffle(P0, cdense[:, :],
                                           [(j + 4) % 8 if j < 8 else j for j in range(32)]), s_v)
                    C.inc(v.tensor_max(P1[0:4, :], cdense[0:4, :], P0[0:4, :]), s_v)
                    C.inc(v.stream_shuffle(P0[0:4, :], P1[0:4, :],
                                           [(j + 2) % 4 if j < 4 else j for j in range(32)]), s_v)
                    C.inc(v.tensor_max(P1[0:2, :], P1[0:2, :], P0[0:2, :]), s_v)
                    C.inc(v.stream_shuffle(P0[0:2, :], P1[0:2, :],
                                           [1 - j if j < 2 else j for j in range(32)]), s_v)
                    C.inc(v.tensor_max(P1[0:1, :], P1[0:1, :], P0[0:1, :]), s_v)

                v_base = C.val(s_v)

                @blk.tensor
                def _(t, v_base=v_base):
                    t.wait_ge(s_v, v_base)
                    for tt in range(4):
                        if tt >= 1:
                            t.wait_ge(s_v, v_base + tt)
                        C.inc(t.matmul(pbc[:, :], onerow[0:1, 0:8],
                                       P1[0:1, tt * 512:(tt + 1) * 512],
                                       start=True, stop=True), s_t)

                @blk.vector
                def _(v):
                    base = C.val(s_t) - 4
                    for tt in range(4):
                        v.wait_ge(s_t, base + tt + 1)
                        sl = slice(tt * 512, (tt + 1) * 512)
                        C.inc(v.tensor_tensor(cdense[:, sl], cdense[:, sl], pbc[:, :],
                                              ALU.subtract), s_v)
            with nc.Block() as blk:
                @blk.scalar
                def _(a):
                    a.wait_ge(s_v, C.val(s_v))
                    C.inc(a.activation(cdense[:, :], cdense[:, :], AF.Exp), s_a)

                @blk.vector
                def _(v):
                    v.wait_ge(s_a, C.val(s_a))
                    C.inc(v.stream_shuffle(P0, cdense[:, :],
                                           [(j + 4) % 8 if j < 8 else j for j in range(32)]), s_v)
                    C.inc(v.tensor_add(P1[0:4, :], cdense[0:4, :], P0[0:4, :]), s_v)
                    C.inc(v.stream_shuffle(P0[0:4, :], P1[0:4, :],
                                           [(j + 2) % 4 if j < 4 else j for j in range(32)]), s_v)
                    C.inc(v.tensor_add(P1[0:2, :], P1[0:2, :], P0[0:2, :]), s_v)
                    C.inc(v.stream_shuffle(P0[0:2, :], P1[0:2, :],
                                           [1 - j if j < 2 else j for j in range(32)]), s_v)
                    C.inc(v.tensor_add(P1[0:1, :], P1[0:1, :], P0[0:1, :]), s_v)
                    C.inc(v.reciprocal(tiny[0:1, 0:2048], P1[0:1, :]), s_v)
                    C.inc(v.memset(P2, 0.0), s_v)
                    for kk in range(1, 8):
                        C.inc(v.stream_shuffle(P0, cdense[:, :],
                                               [(j + kk) % 8 if j < 8 else j for j in range(32)]), s_v)
                        C.inc(v.tensor_tensor(P0, P0, cdense[:, :], ALU.is_gt), s_v)
                        C.inc(v.tensor_add(P2, P2, P0), s_v)
                        C.inc(v.stream_shuffle(P0, cdense[:, :],
                                               [(j + kk) % 8 if j < 8 else j for j in range(32)]), s_v)
                        C.inc(v.tensor_tensor(P0, P0, cdense[:, :], ALU.is_equal), s_v)
                        C.inc(v.tensor_scalar_mul(P0, P0, rankw[:, kk - 1:kk]), s_v)
                        C.inc(v.tensor_add(P2, P2, P0), s_v)
                    C.inc(v.tensor_single_scalar(P0, P2, 2.0, ALU.is_lt), s_v)
                    C.inc(v.tensor_tensor(cdense[:, :], cdense[:, :], P0, ALU.mult), s_v)

                v_base2 = C.val(s_v)

                @blk.tensor
                def _(t, v_base2=v_base2):
                    t.wait_ge(s_v, v_base2)
                    for tt in range(4):
                        if tt >= 1:
                            t.wait_ge(s_v, v_base2 + tt)
                        C.inc(t.matmul(pbc[:, :], onerow[0:1, 0:8],
                                       tiny[0:1, tt * 512:(tt + 1) * 512],
                                       start=True, stop=True), s_t)

                @blk.vector
                def _(v):
                    base = C.val(s_t) - 4
                    for tt in range(4):
                        v.wait_ge(s_t, base + tt + 1)
                        sl = slice(tt * 512, (tt + 1) * 512)
                        C.inc(v.tensor_tensor(cdense[:, sl], cdense[:, sl], pbc[:, :],
                                              ALU.mult), s_v)

        tap("cdense", cdense[:, :], 8, 2048)

        crowz = tiny[0:8, 0:2048]
        with (
            nc.sbuf_tensor(_u("wgu"), [128, 2048], F32) as wgu,
            nc.sbuf_tensor(_u("wdn"), [128, 8192], F32) as wdn,
            nc.sbuf_tensor(_u("gs"), [128, 512], F32) as gs,
            nc.psum_tensor(_u("pg"), [128, 512], F32) as pg,
            nc.psum_tensor(_u("pu"), [128, 512], F32) as pu,
            nc.psum_tensor(_u("pc0"), [128, 512], F32) as pc0,
            nc.psum_tensor(_u("py"), [128, 512], F32) as py,
        ):
            for e in range(8):
                with nc.Block() as blk:
                    @blk.sync
                    def _(s, e=e):
                        s.wait_ge(s_t, C.val(s_t))
                        C.inc(s.dma_start(
                            out=sbap(wdn, 8192, 0, 128, 0, [[1024, 8], [128, 8], [1, 128]]),
                            in_=bass.AP(downB, e * 64 * 128 * 128,
                                        [[128, 128], [8 * 128 * 128, 8],
                                         [128 * 128, 8], [1, 128]])), s_ld, 16)

                    @blk.vector
                    def _(v, e=e):
                        v.wait_ge(s_v, C.val(s_v))
                        C.inc(v.stream_shuffle(crowz[:, :], cdense[:, :],
                                               [e if j < 8 else j for j in range(32)]), s_v)
                for tq in range(4):
                    with nc.Block() as blk:
                        @blk.tensor
                        def _(t, tq=tq):
                            t.wait_ge(s_v, C.val(s_v))
                            C.inc(t.matmul(pc0[:, :], onerow[0:1, :],
                                           crowz[0:1, tq * 512:(tq + 1) * 512],
                                           start=True, stop=True), s_t)
                        for fc in range(8):
                            @blk.sync
                            def _(s, e=e, fc=fc):
                                s.wait_ge(s_t, C.val(s_t))
                                C.inc(s.dma_start(
                                    out=sbap(wgu, 2048, 0, 128, 0, [[128, 8], [1, 128]]),
                                    in_=bass.AP(gateB, (e * 64 + fc * 8) * 128 * 128,
                                                [[128, 128], [128 * 128, 8], [1, 128]])), s_ld, 16)
                                C.inc(s.dma_start(
                                    out=sbap(wgu, 2048, 0, 128, 1024, [[128, 8], [1, 128]]),
                                    in_=bass.AP(upB, (e * 64 + fc * 8) * 128 * 128,
                                                [[128, 128], [128 * 128, 8], [1, 128]])), s_ld, 16)

                            @blk.tensor
                            def _(t, tq=tq):
                                t.wait_ge(s_ld, C.val(s_ld))
                                t.wait_ge(s_v, C.val(s_v))
                                zoff = tq * 512
                                for ic in range(8):
                                    t.matmul(pg[:, :], wgu[:, ic * 128:(ic + 1) * 128],
                                             work[:, ic * TL + zoff: ic * TL + zoff + 512],
                                             start=(ic == 0), stop=(ic == 7))
                                for ic in range(8):
                                    i2 = t.matmul(pu[:, :],
                                                  wgu[:, 1024 + ic * 128: 1024 + (ic + 1) * 128],
                                                  work[:, ic * TL + zoff: ic * TL + zoff + 512],
                                                  start=(ic == 0), stop=(ic == 7))
                                C.inc(i2, s_t)

                            @blk.scalar
                            def _(a):
                                a.wait_ge(s_t, C.val(s_t))
                                C.inc(a.activation(gs[:, :], pg[:, :], AF.Silu), s_a)

                            @blk.vector
                            def _(v, fc=fc):
                                v.wait_ge(s_a, C.val(s_a))
                                C.inc(v.tensor_tensor(gs[:, :], gs[:, :], pu[:, :],
                                                      ALU.mult), s_v)
                                C.inc(v.tensor_tensor(sq[:, fc * 512:(fc + 1) * 512],
                                                      gs[:, :], pc0[:, :], ALU.mult), s_v)

                        for oc in range(8):
                            @blk.tensor
                            def _(t, oc=oc):
                                t.wait_ge(s_v, C.val(s_v))
                                for fc in range(8):
                                    i2 = t.matmul(py[:, :],
                                                  wdn[:, oc * 1024 + fc * 128:
                                                      oc * 1024 + fc * 128 + 128],
                                                  sq[:, fc * 512:(fc + 1) * 512],
                                                  start=(fc == 0), stop=(fc == 7))
                                C.inc(i2, s_t)

                            @blk.vector
                            def _(v, oc=oc, tq=tq):
                                v.wait_ge(s_t, C.val(s_t))
                                dst = xres[:, oc * TL + tq * 512: oc * TL + tq * 512 + 512]
                                C.inc(v.scalar_tensor_tensor(dst, py[:, :], 1.0, dst,
                                                             ALU.mult, ALU.add), s_v)

        with (
            nc.sbuf_tensor(_u("wgu"), [128, 2048], F32) as wgu,
            nc.sbuf_tensor(_u("wdn"), [128, 8192], F32) as wdn,
            nc.sbuf_tensor(_u("gs"), [128, 512], F32) as gs,
            nc.psum_tensor(_u("pg"), [128, 256], F32) as pg,
            nc.psum_tensor(_u("pu"), [128, 256], F32) as pu,
            nc.psum_tensor(_u("py"), [128, 256], F32) as py,
        ):
            for dh in range(2):
                with nc.Block() as blk:
                    @blk.sync
                    def _(s, dh=dh):
                        s.wait_ge(s_t, C.val(s_t))
                        C.inc(s.dma_start(
                            out=sbap(wdn, 8192, 0, 128, 0, [[1024, 8], [128, 8], [1, 128]]),
                            in_=bass.AP(sdownB, dh * 8 * 8 * 128 * 128,
                                        [[128, 128], [8 * 128 * 128, 8],
                                         [128 * 128, 8], [1, 128]])), s_ld, 16)
                for tq in range(8):
                    with nc.Block() as blk:
                        for i8 in range(8):
                            fc = dh * 8 + i8
                            @blk.sync
                            def _(s, fc=fc):
                                s.wait_ge(s_t, C.val(s_t))
                                C.inc(s.dma_start(
                                    out=sbap(wgu, 2048, 0, 128, 0, [[128, 8], [1, 128]]),
                                    in_=bass.AP(sgateB, fc * 8 * 128 * 128,
                                                [[128, 128], [128 * 128, 8], [1, 128]])), s_ld, 16)
                                C.inc(s.dma_start(
                                    out=sbap(wgu, 2048, 0, 128, 1024, [[128, 8], [1, 128]]),
                                    in_=bass.AP(supB, fc * 8 * 128 * 128,
                                                [[128, 128], [128 * 128, 8], [1, 128]])), s_ld, 16)

                            @blk.tensor
                            def _(t, tq=tq):
                                t.wait_ge(s_ld, C.val(s_ld))
                                t.wait_ge(s_v, C.val(s_v))
                                zoff = tq * 256
                                for ic in range(8):
                                    t.matmul(pg[:, :], wgu[:, ic * 128:(ic + 1) * 128],
                                             work[:, ic * TL + zoff: ic * TL + zoff + 256],
                                             start=(ic == 0), stop=(ic == 7))
                                for ic in range(8):
                                    i2 = t.matmul(pu[:, :],
                                                  wgu[:, 1024 + ic * 128: 1024 + (ic + 1) * 128],
                                                  work[:, ic * TL + zoff: ic * TL + zoff + 256],
                                                  start=(ic == 0), stop=(ic == 7))
                                C.inc(i2, s_t)

                            @blk.scalar
                            def _(a):
                                a.wait_ge(s_t, C.val(s_t))
                                C.inc(a.activation(gs[:, 0:256], pg[:, :], AF.Silu), s_a)

                            @blk.vector
                            def _(v, i8=i8):
                                v.wait_ge(s_a, C.val(s_a))
                                C.inc(v.tensor_tensor(gs[:, 0:256], gs[:, 0:256], pu[:, :],
                                                      ALU.mult), s_v)
                                C.inc(v.tensor_copy(sq[:, i8 * 256:(i8 + 1) * 256],
                                                    gs[:, 0:256]), s_v)

                        for oc in range(8):
                            @blk.tensor
                            def _(t, oc=oc):
                                t.wait_ge(s_v, C.val(s_v))
                                for i8 in range(8):
                                    i2 = t.matmul(py[:, :],
                                                  wdn[:, oc * 1024 + i8 * 128:
                                                      oc * 1024 + i8 * 128 + 128],
                                                  sq[:, i8 * 256:(i8 + 1) * 256],
                                                  start=(i8 == 0), stop=(i8 == 7))
                                C.inc(i2, s_t)

                            @blk.vector
                            def _(v, oc=oc, tq=tq):
                                v.wait_ge(s_t, C.val(s_t))
                                dst = xres[:, oc * TL + tq * 256: oc * TL + tq * 256 + 256]
                                C.inc(v.scalar_tensor_tensor(dst, py[:, :], 1.0, dst,
                                                             ALU.mult, ALU.add), s_v)

    if phase_on():
        with nc.sbuf_tensor(_u("cdense1"), [8, 2048], F32) as cdense:
            emit_ffn(cdense)
        tap("x2", xres[:, 0:16384], 128, 16384)

    if phase_on():
        with (
            nc.psum_tensor(_u("pstat1"), [1, 512], F32) as pstat1,
            nc.psum_tensor(_u("pstat2"), [1, 512], F32) as pstat2,
            nc.psum_tensor(_u("pbc1"), [128, 512], F32) as pbc1,
            nc.psum_tensor(_u("pbc2"), [128, 512], F32) as pbc2,
        ):
            for tt in range(4):
                with nc.Block() as blk:
                    ln_tile(blk,
                            lambda c, tt=tt: xres[:, c * TL + tt * 512: c * TL + tt * 512 + 512],
                            lambda c, tt=tt: work[:, c * TL + tt * 512: c * TL + tt * 512 + 512],
                            512, pstat1, pstat2, pbc1, pbc2)

        with (
            nc.sbuf_tensor(_u("wpt"), [128, 8192], F32) as wpt,
            nc.sbuf_tensor(_u("praw"), [128, 4096], F32) as praw,

            nc.psum_tensor(_u("pq0"), [128, 512], F32) as pq0,
            nc.psum_tensor(_u("pstat1"), [1, 512], F32) as pstat1,
            nc.psum_tensor(_u("pstat2"), [1, 512], F32) as pstat2,
            nc.psum_tensor(_u("pbc1"), [128, 512], F32) as pbc1,
            nc.psum_tensor(_u("pn"), [2, 512], F32) as pn,
        ):
            kv1 = sq[:, 0:512]
            kv2 = sq[:, 512:1024]
            bvrow = tiny[0:1, 0:1024]
            bvb = sq[:, 1024:2048]
            for which in ("q", "k"):
                w_hbm = WqT_i if which == "q" else WkT_i
                bcol = 0 if which == "q" else 1
                wcol = 0 if which == "q" else 1
                mcol = 8 if which == "q" else 0
                with nc.Block() as blk:
                    @blk.sync
                    def _(s, w_hbm=w_hbm):
                        s.wait_ge(s_t, C.val(s_t))
                        C.inc(s.dma_start(
                            out=sbap(wpt, 8192, 0, 128, 0, [[1024, 8], [1, 1024]]),
                            in_=bass.AP(w_hbm, 0, [[1024, 128], [128 * 1024, 8], [1, 1024]])),
                            s_ld, 16)

                    @blk.vector
                    def _(v, mcol=mcol):
                        v.wait_ge(s_v, C.val(s_v))
                        C.inc(v.memset(hmax[0:2, mcol:mcol + 8], 0.0), s_v)

                for tt in range(4):
                    with nc.Block() as blk:
                        for oc in range(8):
                            @blk.tensor
                            def _(t, oc=oc, tt=tt):
                                t.wait_ge(s_ld, C.val(s_ld))
                                t.wait_ge(s_a, C.val(s_a))
                                for ic in range(8):
                                    i2 = t.matmul(pq0[:, :],
                                                  wpt[:, ic * 1024 + oc * 128:
                                                      ic * 1024 + oc * 128 + 128],
                                                  work[:, ic * TL + tt * 512:
                                                       ic * TL + tt * 512 + 512],
                                                  start=(ic == 0), stop=(ic == 7))
                                C.inc(i2, s_t)

                            @blk.scalar
                            def _(a, oc=oc, bcol=bcol):
                                a.wait_ge(s_t, C.val(s_t))
                                C.inc(a.add(praw[:, oc * 512:(oc + 1) * 512], pq0[:, :],
                                            bqvt[:, oc * 4 + bcol:oc * 4 + bcol + 1]), s_a)

                        @blk.scalar
                        def _(a):
                            a.wait_ge(s_a, C.val(s_a))
                            C.inc(a.activation(sq[:, 0:4096], praw[:, :], AF.Square), s_a)

                        @blk.tensor
                        def _(t):
                            t.wait_ge(s_a, C.val(s_a))
                            for oc in range(8):
                                i2 = t.matmul(pstat1[0:1, 0:512], onecol[:, :],
                                              sq[:, oc * 512:(oc + 1) * 512],
                                              start=(oc == 0), stop=(oc == 7))
                            C.inc(i2, s_t)

                        @blk.vector
                        def _(v):
                            v.wait_ge(s_t, C.val(s_t))
                            var = tiny[0:1, 512:1024]
                            C.inc(v.tensor_scalar(var, pstat1[0:1, 0:512], 1.0 / 1024.0,
                                                  RMS_EPS, ALU.mult, ALU.add), s_v)
                            C.inc(v.reciprocal(tiny[0:1, 1024:1536], var), s_v)

                        @blk.scalar
                        def _(a):
                            a.wait_ge(s_v, C.val(s_v))
                            C.inc(a.activation(tiny[0:1, 1536:2048], tiny[0:1, 1024:1536],
                                               AF.Sqrt), s_a)

                        @blk.vector
                        def _(v):
                            v.wait_ge(s_a, C.val(s_a))
                            r = tiny[0:1, 1536:2048]
                            var = tiny[0:1, 512:1024]
                            tmp = tiny[0:1, 1024:1536]
                            for _i in range(2):
                                C.inc(v.tensor_tensor(tmp, r, r, ALU.mult), s_v)
                                C.inc(v.tensor_tensor(tmp, var, tmp, ALU.mult), s_v)
                                C.inc(v.tensor_scalar(tmp, tmp, -0.5, 1.5,
                                                      ALU.mult, ALU.add), s_v)
                                C.inc(v.tensor_tensor(r, r, tmp, ALU.mult), s_v)

                        @blk.tensor
                        def _(t):
                            t.wait_ge(s_v, C.val(s_v))
                            C.inc(t.matmul(pbc1[:, 0:512], onerow[0:1, :],
                                           tiny[0:1, 1536:2048],
                                           start=True, stop=True), s_t)

                        for oc in range(8):
                            @blk.vector
                            def _(v, oc=oc, wcol=wcol):
                                v.wait_ge(s_t, C.val(s_t))
                                v.wait_ge(s_st, C.val(s_st))
                                C.inc(v.tensor_tensor(kv1,
                                                      praw[:, oc * 512:(oc + 1) * 512],
                                                      pbc1[:, 0:512], ALU.mult), s_v)
                                C.inc(v.tensor_scalar_mul(kv1, kv1,
                                                          nqkt[:, oc * 2 + wcol:oc * 2 + wcol + 1]),
                                      s_v)

                            @blk.scalar
                            def _(a):
                                a.wait_ge(s_v, C.val(s_v))
                                C.inc(a.activation(kv2, kv1, AF.Square), s_a)

                            @blk.tensor
                            def _(t):
                                t.wait_ge(s_a, C.val(s_a))
                                C.inc(t.matmul(pn[:, :], bones[:, :], kv2,
                                               start=True, stop=True), s_t)

                            @blk.vector
                            def _(v, oc=oc, mcol=mcol):
                                v.wait_ge(s_t, C.val(s_t))
                                C.inc(v.reduce_max(hmax[0:2, 16:17], pn[0:2, :], axis=AX.X), s_v)
                                C.inc(v.tensor_max(hmax[0:2, mcol + oc:mcol + oc + 1],
                                                   hmax[0:2, mcol + oc:mcol + oc + 1],
                                                   hmax[0:2, 16:17]), s_v)

                            @blk.gpsimd
                            def _(g, oc=oc, tt=tt, which=which):
                                g.wait_ge(s_v, C.val(s_v))
                                if which == "q":
                                    C.inc(g.dma_start(
                                        out=bass.AP(qbuf, oc * 128 * TL + tt * 512,
                                                    [[TL, 128], [1, 512]]),
                                        in_=kv1), s_st, 16)
                                else:
                                    C.inc(g.dma_start(
                                        out=bass.AP(cin, CIN_K + oc * 128 * TL + tt * 512,
                                                    [[TL, 128], [1, 512]]),
                                        in_=kv1), s_st, 16)

            with nc.Block() as blk:
                @blk.gpsimd
                def _(g):
                    g.wait_ge(s_v, C.val(s_v))
                    C.inc(g.dma_start(out=bass.AP(cin, CIN_M, [[8, 2], [1, 8]]),
                                      in_=hmax[0:2, 0:8]), s_st, 16)

            with nc.Block() as blk:
                @blk.sync
                def _(s):
                    s.wait_ge(s_t, C.val(s_t))
                    s.wait_ge(s_v, C.val(s_v))
                    s.wait_ge(s_a, C.val(s_a))
                    C.inc(s.dma_start(out=bvrow, in_=bvrow_i.ap()), s_ld, 16)
                    C.inc(s.dma_start(
                        out=sbap(wpt, 8192, 0, 128, 0, [[1024, 8], [1, 1024]]),
                        in_=bass.AP(WvT_i, 0, [[1024, 128], [128 * 1024, 8], [1, 1024]])),
                        s_ld, 16)


            with nc.Block() as blk:
                @blk.tensor
                def _(t):
                    t.wait_ge(s_v, C.val(s_v))
                    C.inc(t.matmul(pbc1[:, 0:512], onerow[0:1, :], bvrow[0:1, 0:512],
                                   start=True, stop=True), s_t)

                @blk.vector
                def _(v):
                    v.wait_ge(s_t, C.val(s_t))
                    C.inc(v.tensor_copy(bvb[:, 0:512], pbc1[:, 0:512]), s_v)

                @blk.tensor
                def _(t):
                    t.wait_ge(s_v, C.val(s_v))
                    C.inc(t.matmul(pbc1[:, 0:512], onerow[0:1, :], bvrow[0:1, 512:1024],
                                   start=True, stop=True), s_t)

                @blk.vector
                def _(v):
                    v.wait_ge(s_t, C.val(s_t))
                    C.inc(v.tensor_copy(bvb[:, 512:1024], pbc1[:, 0:512]), s_v)

            for tt in range(16):
                for nt in range(2):
                    with nc.Block() as blk:
                        @blk.tensor
                        def _(t, tt=tt, nt=nt):
                            t.wait_ge(s_ld, C.val(s_ld))
                            t.wait_ge(s_st, C.val(s_st))
                            for ic in range(8):
                                i2 = t.matmul(pq0[:, :],
                                              work[:, ic * TL + tt * 128: ic * TL + tt * 128 + 128],
                                              wpt[:, ic * 1024 + nt * 512:
                                                  ic * 1024 + nt * 512 + 512],
                                              start=(ic == 0), stop=(ic == 7))
                            C.inc(i2, s_t)

                        @blk.vector
                        def _(v, nt=nt):
                            v.wait_ge(s_t, C.val(s_t))
                            C.inc(v.tensor_tensor(kv1, pq0[:, :],
                                                  bvb[:, nt * 512:(nt + 1) * 512], ALU.add), s_v)

                        @blk.gpsimd
                        def _(g, tt=tt, nt=nt):
                            g.wait_ge(s_v, C.val(s_v))
                            for i in range(4):
                                hp = nt * 4 + i
                                C.inc(g.dma_start(
                                    out=bass.AP(cin, CIN_V + hp * TL * 128 + tt * 128 * 128,
                                                [[128, 128], [1, 128]]),
                                    in_=kv1[:, i * 128:(i + 1) * 128]), s_st, 16)

        with nc.Block() as blk:
            @blk.gpsimd
            def _(g):
                g.wait_ge(s_st, C.val(s_st))
                C.inc(g.collective_compute(
                    "AllGather", ALU.bypass,
                    replica_groups=[[0, 1], [2, 3], [4, 5], [6, 7]],
                    ins=[cin.ap().opt()], outs=[cout.ap().opt()]), s_g)
                g.wait_ge(s_g, C.val(s_g))

            @blk.sync
            def _(s):
                s.wait_ge(s_g, C.val(s_g))
                C.inc(s.dma_start(out=hmax[0:2, 0:8],
                                  in_=bass.AP(cout, CIN_M, [[8, 2], [1, 8]])), s_ld, 16)
                C.inc(s.dma_start(out=hmax[0:2, 16 - 16:2 * 8 - 8] if False else sq[0:2, 0:8],
                                  in_=bass.AP(cout, CIN_N + CIN_M, [[8, 2], [1, 8]])), s_ld, 16)

            @blk.vector
            def _(v):
                v.wait_ge(s_ld, C.val(s_ld))
                C.inc(v.tensor_max(hmax[0:2, 0:8], hmax[0:2, 0:8], sq[0:2, 0:8]), s_v)

        bias_assembly()
        emit_attention(causal=False)
        tap("attnO2", work[:, 0:16384], 128, 16384)
        emit_out_proj(WoT_i, with_bias=True, load_x=False)
        tap("x3", xres[:, 0:16384], 128, 16384)

    if phase_on():
        with nc.sbuf_tensor(_u("cdense2"), [8, 2048], F32) as cdense:
            emit_ffn(cdense)
        tap("x4", xres[:, 0:16384], 128, 16384)

    if phase_on():
        with nc.Block() as blk:
            @blk.gpsimd
            def _(g):
                g.wait_ge(s_v, C.val(s_v))
                g.wait_ge(s_t, C.val(s_t))
                C.inc(g.dma_start(
                    out=bass.AP(outT, 0, [[TL, 128], [128 * TL, 8], [1, TL]]),
                    in_=sbap(xres, 8 * TL, 0, 128, 0, [[TL, 8], [1, TL]])), s_st, 16)
                g.wait_ge(s_st, C.val(s_st))

    es.close()
    return nc


def _rope_tables():
    inv = 1.0 / (10000.0 ** (np.arange(0, 64, 2, dtype=np.float32) / 64))
    ang = np.outer(np.arange(T, dtype=np.float32), inv)
    cos, sin = np.cos(ang).astype(np.float32), np.sin(ang).astype(np.float32)
    cdup = np.zeros((128, T), np.float32)
    sdup = np.zeros((128, T), np.float32)
    for r in range(128):
        i = (r % 64) // 2
        cdup[r] = cos[:, i]
        sdup[r] = (sin[:, i] if (r % 2 == 1) else -sin[:, i])
    return cdup, sdup


def _token_idx(h):
    j = np.arange(TL)
    return (j // 128) * 256 + h * 128 + (j % 128)


_BUILT = [None]


def kernel(**inputs):
    x0 = np.asarray(inputs["L0_attn_a_x"])
    cdup, sdup = _rope_tables()
    bones = np.zeros((128, 2), np.float32)
    bones[0:64, 0] = 1.0
    bones[64:128, 1] = 1.0
    rankwm = np.zeros((8, 7), np.float32)
    for e in range(8):
        for kk in range(1, 8):
            rankwm[e, kk - 1] = 1.0 if (e + kk) % 8 < e else 0.0

    gateBm = np.ascontiguousarray(
        np.asarray(inputs["L0_ffn_gate_w"]).reshape(8, 8, 128, 8, 128).transpose(0, 3, 1, 2, 4))
    upBm = np.ascontiguousarray(
        np.asarray(inputs["L0_ffn_up_w"]).reshape(8, 8, 128, 8, 128).transpose(0, 3, 1, 2, 4))
    downBm = np.ascontiguousarray(
        np.asarray(inputs["L0_ffn_down_w"]).reshape(8, 8, 128, 8, 128).transpose(0, 3, 1, 2, 4))
    sgateBm = np.ascontiguousarray(
        np.asarray(inputs["L0_ffn_shared_gate_w"]).reshape(8, 128, 16, 128).transpose(2, 0, 1, 3))
    supBm = np.ascontiguousarray(
        np.asarray(inputs["L0_ffn_shared_up_w"]).reshape(8, 128, 16, 128).transpose(2, 0, 1, 3))
    sdownBm = np.ascontiguousarray(
        np.asarray(inputs["L0_ffn_shared_down_w"]).reshape(2, 8, 128, 8, 128).transpose(0, 3, 1, 2, 4))
    bqvm = np.stack([np.asarray(inputs["L0_attn_b_bq"]), np.asarray(inputs["L0_attn_b_bk"]),
                     np.asarray(inputs["L0_attn_b_bv"]), np.asarray(inputs["L0_attn_b_bo"])],
                    axis=1).astype(np.float32)
    nqkm = np.stack([np.asarray(inputs["L0_attn_b_norm_q_w"]),
                     np.asarray(inputs["L0_attn_b_norm_k_w"])], axis=1).astype(np.float32)

    shared = {
        "wq": np.asarray(inputs["L0_attn_a_wq"]),
        "w_down": np.asarray(inputs["L0_attn_a_w_down"]),
        "wk_up": np.asarray(inputs["L0_attn_a_wk_up"]),
        "wv_up": np.asarray(inputs["L0_attn_a_wv_up"]),
        "wo": np.asarray(inputs["L0_attn_a_wo"]),
        "WqT": np.ascontiguousarray(np.asarray(inputs["L0_attn_b_Wq"]).T),
        "WkT": np.ascontiguousarray(np.asarray(inputs["L0_attn_b_Wk"]).T),
        "WvT": np.ascontiguousarray(np.asarray(inputs["L0_attn_b_Wv"]).T),
        "WoT": np.ascontiguousarray(np.asarray(inputs["L0_attn_b_Wo"]).T),
        "bqv": bqvm, "nqk": nqkm,
        "bvrow_in": np.ascontiguousarray(np.asarray(inputs["L0_attn_b_bv"]).reshape(1, 1024)),
        "router_w": np.asarray(inputs["L0_ffn_router_w"]),
        "gateB": gateBm, "upB": upBm, "downB": downBm,
        "sgateB": sgateBm, "supB": supBm, "sdownB": sdownBm,
        "cdup_in": cdup, "sdup_in": sdup, "bones": bones, "rankw": rankwm,
    }

    in_maps = []
    idxs = []
    for c in range(8):
        b, h = c // 2, c % 2
        idx = _token_idx(h)
        idxs.append(idx)
        xseqTm = np.ascontiguousarray(x0[b].T)
        m = dict(shared)
        m["xseqT"] = xseqTm
        m["xmyT"] = np.ascontiguousarray(xseqTm[:, idx])
        m["cdupQ"] = np.ascontiguousarray(cdup[:, idx])
        m["sdupQ"] = np.ascontiguousarray(sdup[:, idx])
        in_maps.append(m)

    if _BUILT[0] is None:
        _BUILT[0] = build()
    nc = _BUILT[0]
    res = run_bass_kernel_spmd(nc, in_maps, core_ids=list(range(8)))

    out = np.empty((B, T, E), np.float32)
    for c in range(8):
        b, h = c // 2, c % 2
        out[b, idxs[c], :] = res.results[c]["outT"].T
    return out


# version 03 (final)
# speedup vs baseline: 1.1084x; 1.1084x over previous
import numpy as np
from contextlib import ExitStack

import concourse.bass as bass
import concourse.mybir as mybir
from concourse.bass_utils import run_bass_kernel_spmd

F32 = mybir.dt.float32
AF = mybir.ActivationFunctionType
ALU = mybir.AluOpType
AX = mybir.AxisListType

B, T, E = 4, 4096, 1024
PAGE = 256
TL = 2048
LN_EPS = 1e-5
RMS_EPS = 1e-6
SCALE = 0.125


class Counter:
    def __init__(self):
        self.n = {}

    def inc(self, inst, sem, k=1):
        inst.then_inc(sem, k)
        self.n[sem] = self.n.get(sem, 0) + k
        return self.n[sem]

    def val(self, sem):
        return self.n.get(sem, 0)


def sbap(t, pitch, p0, npar, off, dims):
    return bass.AP(t, p0 * pitch + off, [[pitch, npar]] + dims)


def build(upto=99, taps=None):
    taps = taps or {}
    nc = bass.Bass("TRN2", target_bir_lowering=False)
    C = Counter()

    def din(name, shape):
        return nc.dram_tensor(name, shape, F32, kind="ExternalInput")

    xmyT = din("xmyT", [E, TL])
    xseqT = din("xseqT", [E, T])
    wq_i = din("wq", [E, E])
    wdown_i = din("w_down", [E, 512])
    wkup_i = din("wk_up", [512, E])
    wvup_i = din("wv_up", [512, E])
    wo_i = din("wo", [E, E])
    WqT_i = din("WqT", [E, E])
    WkT_i = din("WkT", [E, E])
    WvT_i = din("WvT", [E, E])
    WoT_i = din("WoT", [E, E])
    bqv_i = din("bqv", [E, 4])
    nqk_i = din("nqk", [E, 2])
    rout_i = din("router_w", [E, 8])
    gateB = din("gateB", [8, 8, 8, 128, 128])
    upB = din("upB", [8, 8, 8, 128, 128])
    downB = din("downB", [8, 8, 8, 128, 128])
    sgateB = din("sgateB", [16, 8, 128, 128])
    supB = din("supB", [16, 8, 128, 128])
    sdownB = din("sdownB", [2, 8, 8, 128, 128])
    cdup_i = din("cdup_in", [128, T])
    sdup_i = din("sdup_in", [128, T])
    cdupQ_i = din("cdupQ", [128, TL])
    sdupQ_i = din("sdupQ", [128, TL])
    bones_i = din("bones", [128, 2])
    rankw_i = din("rankw", [8, 7])
    bvrow_i = din("bvrow_in", [1, 1024])
    outT = nc.dram_tensor("outT", [E, TL], F32, kind="ExternalOutput")

    kbuf = nc.dram_tensor("kbuf", [E, T], F32)
    vbuf = nc.dram_tensor("vbuf", [8, T, 128], F32)
    qbuf = nc.dram_tensor("qbuf", [E, TL], F32)
    CIN_K = 0
    CIN_V = E * TL
    CIN_M = CIN_V + 8 * TL * 128
    CIN_N = CIN_M + 128
    cin = nc.dram_tensor("cin", [CIN_N], F32)
    cout = nc.dram_tensor("cout", [2, CIN_N], F32)

    es = ExitStack()
    uid = [0]

    def _u(n):
        uid[0] += 1
        return f"{n}_u{uid[0]}"

    sb = lambda name, p, n: es.enter_context(nc.sbuf_tensor(name, [p, n], F32))
    xres = sb("xres", 128, 8 * TL)
    work = sb("work", 128, 16384)
    bones = sb("bonest", 128, 2)
    onecol = sb("onecol", 128, 1)
    onerow = sb("onerow", 1, 128)
    tiny = sb("tiny", 16, 2048)
    sq = sb("sqbuf", 128, 4096)
    hmax = sb("hmax", 2, 18)
    biasall = sb("biasall", 128, 16)
    rankw = sb("rankwt", 8, 7)
    bqvt = sb("bqvt", 128, 8 * 4)
    nqkt = sb("nqkt", 128, 8 * 2)
    routw = sb("routwt", 128, 8 * 8)

    s_ld = es.enter_context(nc.semaphore("s_ld"))
    s_st = es.enter_context(nc.semaphore("s_st"))
    s_t = es.enter_context(nc.semaphore("s_t"))
    s_v = es.enter_context(nc.semaphore("s_v"))
    s_a = es.enter_context(nc.semaphore("s_a"))
    s_g = es.enter_context(nc.semaphore("s_g"))
    s_x = es.enter_context(nc.semaphore("s_x"))

    phase = [0]

    def phase_on():
        phase[0] += 1
        return phase[0] <= upto

    def tap_dram(name, dram_t, shape):
        if name not in taps:
            return
        tout = nc.dram_tensor("tap_" + name, shape, F32, kind="ExternalOutput")
        n = 1
        for s in shape:
            n *= s
        with nc.Block() as blk:
            @blk.gpsimd
            def _(g):
                g.wait_ge(s_st, C.val(s_st))
                g.wait_ge(s_v, C.val(s_v))
                C.inc(g.dma_start(out=bass.AP(tout, 0, [[n, 1], [1, n]]),
                                  in_=bass.AP(dram_t, 0, [[n, 1], [1, n]])), s_st, 16)
                g.wait_ge(s_st, C.val(s_st))

    def tap(name, src_ap, rows, cols):
        if name not in taps:
            return
        tout = nc.dram_tensor("tap_" + name, [rows, cols], F32, kind="ExternalOutput")
        with nc.Block() as blk:
            @blk.gpsimd
            def _(g):
                g.wait_ge(s_t, C.val(s_t))
                g.wait_ge(s_v, C.val(s_v))
                g.wait_ge(s_a, C.val(s_a))
                C.inc(g.dma_start(out=tout.ap(), in_=src_ap), s_st, 16)
                g.wait_ge(s_st, C.val(s_st))

    if phase_on():
        with nc.Block() as blk:
            @blk.sync
            def _(s):
                C.inc(s.dma_start(out=bones[:, :], in_=bones_i.ap()), s_ld, 16)
                C.inc(s.dma_start(out=rankw[:, :], in_=rankw_i.ap()), s_ld, 16)
                C.inc(s.dma_start(
                    out=sbap(bqvt, 32, 0, 128, 0, [[4, 8], [1, 4]]),
                    in_=bass.AP(bqv_i, 0, [[4, 128], [128 * 4, 8], [1, 4]])), s_ld, 16)
                C.inc(s.dma_start(
                    out=sbap(nqkt, 16, 0, 128, 0, [[2, 8], [1, 2]]),
                    in_=bass.AP(nqk_i, 0, [[2, 128], [128 * 2, 8], [1, 2]])), s_ld, 16)
                C.inc(s.dma_start(
                    out=sbap(routw, 64, 0, 128, 0, [[8, 8], [1, 8]]),
                    in_=bass.AP(rout_i, 0, [[8, 128], [128 * 8, 8], [1, 8]])), s_ld, 16)

            @blk.vector
            def _(v):
                v.wait_ge(s_ld, C.val(s_ld))
                C.inc(v.memset(onecol[:, :], 1.0), s_v)
                C.inc(v.memset(onerow[:, :], 1.0), s_v)

    def ln_tile(blk, src_fn, dst_fn, ntok, st1, st2, bc1, bc2, eps=LN_EPS):
        m = tiny[0:1, 0:ntok]
        var = tiny[0:1, 512:512 + ntok]
        tmp = tiny[0:1, 1024:1024 + ntok]
        r = tiny[0:1, 1536:1536 + ntok]

        @blk.scalar
        def _(a):
            a.wait_ge(s_ld, C.val(s_ld))
            a.wait_ge(s_v, C.val(s_v))
            for c in range(8):
                i2 = a.activation(sq[:, c * ntok:(c + 1) * ntok], src_fn(c), AF.Square)
            C.inc(i2, s_a)

        @blk.tensor
        def _(t):
            t.wait_ge(s_a, C.val(s_a))
            t.wait_ge(s_v, C.val(s_v))
            for c in range(8):
                t.matmul(st1[0:1, 0:ntok], onecol[:, :], src_fn(c),
                         start=(c == 0), stop=(c == 7))
            for c in range(8):
                i2 = t.matmul(st2[0:1, 0:ntok], onecol[:, :],
                              sq[:, c * ntok:(c + 1) * ntok],
                              start=(c == 0), stop=(c == 7))
            C.inc(i2, s_t)

        @blk.vector
        def _(v):
            v.wait_ge(s_t, C.val(s_t))
            C.inc(v.tensor_scalar_mul(m, st1[0:1, 0:ntok], 1.0 / 1024.0), s_v)
            C.inc(v.tensor_tensor(tmp, m, m, ALU.mult), s_v)
            C.inc(v.scalar_tensor_tensor(var, st2[0:1, 0:ntok], 1.0 / 1024.0,
                                         tmp, ALU.mult, ALU.subtract), s_v)
            C.inc(v.tensor_scalar_add(var, var, eps), s_v)
            C.inc(v.reciprocal(tmp, var), s_v)

        @blk.scalar
        def _(a):
            a.wait_ge(s_v, C.val(s_v))
            C.inc(a.activation(r, tmp, AF.Sqrt), s_a)

        @blk.vector
        def _(v):
            v.wait_ge(s_a, C.val(s_a))
            for _i in range(2):
                C.inc(v.tensor_tensor(tmp, r, r, ALU.mult), s_v)
                C.inc(v.tensor_tensor(tmp, var, tmp, ALU.mult), s_v)
                C.inc(v.tensor_scalar(tmp, tmp, -0.5, 1.5, ALU.mult, ALU.add), s_v)
                C.inc(v.tensor_tensor(r, r, tmp, ALU.mult), s_v)
            C.inc(v.tensor_tensor(m, m, r, ALU.mult), s_v)

        @blk.tensor
        def _(t):
            t.wait_ge(s_v, C.val(s_v))
            t.matmul(bc1[:, 0:ntok], onerow[0:1, :], r, start=True, stop=True)
            C.inc(t.matmul(bc2[:, 0:ntok], onerow[0:1, :], m, start=True, stop=True), s_t)

        @blk.vector
        def _(v):
            v.wait_ge(s_t, C.val(s_t))
            for c in range(8):
                v.tensor_tensor(dst_fn(c), src_fn(c), bc1[:, 0:ntok], ALU.mult)
                i2 = v.tensor_tensor(dst_fn(c), dst_fn(c), bc2[:, 0:ntok], ALU.subtract)
            C.inc(i2, s_v)

    if phase_on():
        with (
            nc.sbuf_tensor(_u("wdt"), [128, 8 * 512], F32) as wdt,
            nc.sbuf_tensor(_u("xs1"), [128, 8 * 512], F32) as xs1,
            nc.psum_tensor(_u("pstat1"), [1, 512], F32) as pstat1,
            nc.psum_tensor(_u("pstat2"), [1, 512], F32) as pstat2,
            nc.psum_tensor(_u("pbc1"), [128, 512], F32) as pbc1,
            nc.psum_tensor(_u("pbc2"), [128, 512], F32) as pbc2,
            nc.psum_tensor(_u("plat"), [128, 512], F32) as plat,
        ):
            with nc.Block() as blk:
                @blk.sync
                def _(s):
                    C.inc(s.dma_start(
                        out=sbap(wdt, 8 * 512, 0, 128, 0, [[512, 8], [1, 512]]),
                        in_=bass.AP(wdown_i, 0, [[512, 128], [128 * 512, 8], [1, 512]])),
                        s_ld, 16)
            for tt in range(8):
                with nc.Block() as blk:
                    @blk.sync
                    def _(s, tt=tt):
                        s.wait_ge(s_t, C.val(s_t))
                        s.wait_ge(s_v, C.val(s_v))
                        s.wait_ge(s_a, C.val(s_a))
                        C.inc(s.dma_start(
                            out=sbap(xs1, 8 * 512, 0, 128, 0, [[512, 8], [1, 512]]),
                            in_=bass.AP(xseqT, tt * 512, [[T, 128], [128 * T, 8], [1, 512]])),
                            s_ld, 16)
                    ln_tile(blk,
                            lambda c: xs1[:, c * 512:(c + 1) * 512],
                            lambda c: xs1[:, c * 512:(c + 1) * 512],
                            512, pstat1, pstat2, pbc1, pbc2)
                    for oc in range(4):
                        @blk.tensor
                        def _(t, oc=oc):
                            t.wait_ge(s_v, C.val(s_v))
                            for ic in range(8):
                                i2 = t.matmul(plat[:, :],
                                              wdt[:, ic * 512 + oc * 128: ic * 512 + oc * 128 + 128],
                                              xs1[:, ic * 512:(ic + 1) * 512],
                                              start=(ic == 0), stop=(ic == 7))
                            C.inc(i2, s_t)

                        @blk.vector
                        def _(v, oc=oc, tt=tt):
                            v.wait_ge(s_t, C.val(s_t))
                            C.inc(v.tensor_copy(
                                work[:, oc * 4096 + tt * 512: oc * 4096 + tt * 512 + 512],
                                plat[:, :]), s_v)
        tap("lat", work[:, 0:16384], 128, 16384)

    if phase_on():
        with (
            nc.sbuf_tensor(_u("wkv"), [128, 4096], F32) as wkv,
            nc.sbuf_tensor(_u("cdup"), [128, 512], F32) as cdup,
            nc.sbuf_tensor(_u("sdup"), [128, 512], F32) as sdup,
            nc.psum_tensor(_u("pk0"), [128, 512], F32) as pk0,
            nc.psum_tensor(_u("pn"), [2, 512], F32) as pn,
        ):
            kv1 = sq[:, 0:512]
            kv2 = sq[:, 512:1024]
            kv3 = sq[:, 1024:1536]
            with nc.Block() as blk:
                @blk.sync
                def _(s):
                    C.inc(s.dma_start(
                        out=sbap(wkv, 4096, 0, 128, 0, [[1024, 4], [1, 1024]]),
                        in_=bass.AP(wkup_i, 0, [[1024, 128], [128 * 1024, 4], [1, 1024]])),
                        s_ld, 16)


                @blk.vector
                def _(v):
                    C.inc(v.memset(hmax[:, :], 0.0), s_v)

            for oc in range(8):
                for tt in range(8):
                    with nc.Block() as blk:
                        @blk.sync
                        def _(s, tt=tt):
                            s.wait_ge(s_v, C.val(s_v))
                            C.inc(s.dma_start(out=cdup[:, :],
                                              in_=cdup_i.ap()[:, tt * 512:(tt + 1) * 512]), s_ld, 16)
                            C.inc(s.dma_start(out=sdup[:, :],
                                              in_=sdup_i.ap()[:, tt * 512:(tt + 1) * 512]), s_ld, 16)

                        @blk.tensor
                        def _(t, oc=oc, tt=tt):
                            t.wait_ge(s_ld, C.val(s_ld))
                            t.wait_ge(s_st, C.val(s_st))
                            for ic in range(4):
                                i2 = t.matmul(pk0[:, :],
                                              wkv[:, ic * 1024 + oc * 128: ic * 1024 + oc * 128 + 128],
                                              work[:, ic * 4096 + tt * 512: ic * 4096 + tt * 512 + 512],
                                              start=(ic == 0), stop=(ic == 3))
                            C.inc(i2, s_t)

                        @blk.vector
                        def _(v, tt=tt):
                            v.wait_ge(s_t, C.val(s_t))
                            C.inc(v.tensor_copy(kv2, pk0[:, :]), s_v)
                            C.inc(v.stream_shuffle(kv3, kv2,
                                                   [j ^ 1 for j in range(32)]), s_v)
                            C.inc(v.tensor_tensor(kv1, kv2,
                                                  cdup[:, :], ALU.mult), s_v)
                            C.inc(v.tensor_tensor(kv3, kv3,
                                                  sdup[:, :], ALU.mult), s_v)
                            C.inc(v.tensor_add(kv1, kv1, kv3), s_v)

                        @blk.scalar
                        def _(a):
                            a.wait_ge(s_v, C.val(s_v))
                            C.inc(a.activation(kv2, kv1, AF.Square), s_a)

                        @blk.tensor
                        def _(t):
                            t.wait_ge(s_a, C.val(s_a))
                            C.inc(t.matmul(pn[:, :], bones[:, :], kv2,
                                           start=True, stop=True), s_t)

                        @blk.vector
                        def _(v, oc=oc):
                            v.wait_ge(s_t, C.val(s_t))
                            C.inc(v.reduce_max(hmax[0:2, 16:17], pn[0:2, :], axis=AX.X), s_v)
                            C.inc(v.tensor_max(hmax[0:2, oc:oc + 1], hmax[0:2, oc:oc + 1],
                                               hmax[0:2, 16:17]), s_v)

                        @blk.gpsimd
                        def _(g, oc=oc, tt=tt):
                            g.wait_ge(s_v, C.val(s_v))
                            C.inc(g.dma_start(
                                out=bass.AP(kbuf, oc * 128 * T + tt * 512, [[T, 128], [1, 512]]),
                                in_=kv1), s_st, 16)

            with nc.Block() as blk:
                @blk.sync
                def _(s):
                    s.wait_ge(s_t, C.val(s_t))
                    C.inc(s.dma_start(
                        out=sbap(wkv, 4096, 0, 128, 0, [[1024, 4], [1, 1024]]),
                        in_=bass.AP(wvup_i, 0, [[1024, 128], [128 * 1024, 4], [1, 1024]])),
                        s_ld, 16)
            for tt in range(32):
                for nt in range(2):
                    with nc.Block() as blk:
                        @blk.tensor
                        def _(t, tt=tt, nt=nt):
                            t.wait_ge(s_ld, C.val(s_ld))
                            t.wait_ge(s_st, C.val(s_st))
                            for ic in range(4):
                                i2 = t.matmul(pk0[:, :],
                                              work[:, ic * 4096 + tt * 128: ic * 4096 + tt * 128 + 128],
                                              wkv[:, ic * 1024 + nt * 512: ic * 1024 + nt * 512 + 512],
                                              start=(ic == 0), stop=(ic == 3))
                            C.inc(i2, s_t)

                        @blk.vector
                        def _(v):
                            v.wait_ge(s_t, C.val(s_t))
                            C.inc(v.tensor_copy(kv1, pk0[:, :]), s_v)

                        @blk.gpsimd
                        def _(g, tt=tt, nt=nt):
                            g.wait_ge(s_v, C.val(s_v))
                            for i in range(4):
                                hp = nt * 4 + i
                                C.inc(g.dma_start(
                                    out=bass.AP(vbuf, hp * T * 128 + tt * 128 * 128,
                                                [[128, 128], [1, 128]]),
                                    in_=kv1[:, i * 128:(i + 1) * 128]), s_st, 16)
        tap_dram("kbuf", kbuf, [E, T])
        tap_dram("vbuf", vbuf, [8 * T, 128])

    if phase_on():
        with (
            nc.sbuf_tensor(_u("cdq"), [128, TL], F32) as cdq,
            nc.sbuf_tensor(_u("sdq"), [128, TL], F32) as sdq,
            nc.psum_tensor(_u("pstat1"), [1, 512], F32) as pstat1,
            nc.psum_tensor(_u("pstat2"), [1, 512], F32) as pstat2,
            nc.psum_tensor(_u("pbc1"), [128, 512], F32) as pbc1,
            nc.psum_tensor(_u("pbc2"), [128, 512], F32) as pbc2,
            nc.psum_tensor(_u("pq0"), [128, 512], F32) as pq0,
            nc.psum_tensor(_u("pn"), [2, 512], F32) as pn,
        ):
            xs1 = work[:, 0:4096]
            wqt = work[:, 4096:12288]
            kv1 = work[:, 12288:12800]
            kv2 = work[:, 12800:13312]
            kv3 = work[:, 13312:13824]
            with nc.Block() as blk:
                @blk.sync
                def _(s):
                    s.wait_ge(s_t, C.val(s_t))
                    C.inc(s.dma_start(
                        out=bass.AP(work, 4096, [[16384, 128], [1024, 8], [1, 1024]]),
                        in_=bass.AP(wq_i, 0, [[1024, 128], [128 * 1024, 8], [1, 1024]])), s_ld, 16)
                    C.inc(s.dma_start(out=cdq[:, :], in_=cdupQ_i.ap()), s_ld, 16)
                    C.inc(s.dma_start(out=sdq[:, :], in_=sdupQ_i.ap()), s_ld, 16)
            for tt in range(4):
                with nc.Block() as blk:
                    @blk.sync
                    def _(s, tt=tt):
                        s.wait_ge(s_t, C.val(s_t))
                        s.wait_ge(s_v, C.val(s_v))
                        s.wait_ge(s_a, C.val(s_a))
                        C.inc(s.dma_start(
                            out=bass.AP(work, 0, [[16384, 128], [512, 8], [1, 512]]),
                            in_=bass.AP(xmyT, tt * 512, [[TL, 128], [128 * TL, 8], [1, 512]])),
                            s_ld, 16)
                    ln_tile(blk,
                            lambda c: work[:, c * 512:(c + 1) * 512],
                            lambda c: work[:, c * 512:(c + 1) * 512],
                            512, pstat1, pstat2, pbc1, pbc2)
                    for oc in range(8):
                        @blk.tensor
                        def _(t, oc=oc):
                            t.wait_ge(s_v, C.val(s_v))
                            t.wait_ge(s_st, C.val(s_st))
                            for ic in range(8):
                                i2 = t.matmul(pq0[:, :],
                                              work[:, 4096 + ic * 1024 + oc * 128:
                                                   4096 + ic * 1024 + oc * 128 + 128],
                                              work[:, ic * 512:(ic + 1) * 512],
                                              start=(ic == 0), stop=(ic == 7))
                            C.inc(i2, s_t)

                        @blk.vector
                        def _(v, tt=tt):
                            v.wait_ge(s_t, C.val(s_t))
                            C.inc(v.tensor_copy(kv2, pq0[:, :]), s_v)
                            C.inc(v.stream_shuffle(kv3, kv2,
                                                   [j ^ 1 for j in range(32)]), s_v)
                            C.inc(v.tensor_tensor(kv1, kv2,
                                                  cdq[:, tt * 512:(tt + 1) * 512], ALU.mult), s_v)
                            C.inc(v.tensor_tensor(kv3, kv3,
                                                  sdq[:, tt * 512:(tt + 1) * 512], ALU.mult), s_v)
                            C.inc(v.tensor_add(kv1, kv1, kv3), s_v)

                        @blk.scalar
                        def _(a):
                            a.wait_ge(s_v, C.val(s_v))
                            C.inc(a.activation(kv2, kv1, AF.Square), s_a)

                        @blk.tensor
                        def _(t):
                            t.wait_ge(s_a, C.val(s_a))
                            C.inc(t.matmul(pn[:, :], bones[:, :], kv2,
                                           start=True, stop=True), s_t)

                        @blk.vector
                        def _(v, oc=oc):
                            v.wait_ge(s_t, C.val(s_t))
                            C.inc(v.reduce_max(hmax[0:2, 16:17], pn[0:2, :], axis=AX.X), s_v)
                            C.inc(v.tensor_max(hmax[0:2, 8 + oc:9 + oc], hmax[0:2, 8 + oc:9 + oc],
                                               hmax[0:2, 16:17]), s_v)

                        @blk.gpsimd
                        def _(g, oc=oc, tt=tt):
                            g.wait_ge(s_v, C.val(s_v))
                            C.inc(g.dma_start(
                                out=bass.AP(qbuf, oc * 128 * TL + tt * 512, [[TL, 128], [1, 512]]),
                                in_=kv1), s_st, 16)

    if True:
        tap_dram("qbuf", qbuf, [E, TL])

    def bias_assembly():
        with (
            nc.sbuf_tensor(_u("mrow"), [2, 8], F32) as mrow,
            nc.psum_tensor(_u("pbias"), [128, 8], F32) as pbias,
            nc.Block() as blk,
        ):
            @blk.vector
            def _(v):
                v.wait_ge(s_v, C.val(s_v))
                i2 = None
                for oc in range(8):
                    i2 = v.tensor_tensor(mrow[0:2, oc:oc + 1], hmax[0:2, oc:oc + 1],
                                         hmax[0:2, 8 + oc:9 + oc], ALU.mult)
                C.inc(i2, s_v)

            @blk.scalar
            def _(a):
                a.wait_ge(s_v, C.val(s_v))
                C.inc(a.activation(mrow[0:2, 0:8], mrow[0:2, 0:8], AF.Sqrt), s_a)

            @blk.vector
            def _(v):
                v.wait_ge(s_a, C.val(s_a))
                C.inc(v.tensor_scalar_mul(mrow[0:2, 0:8], mrow[0:2, 0:8],
                                          -SCALE * 1.02), s_v)
                C.inc(v.stream_shuffle(sq[0:2, 0:8], mrow[0:2, 0:8],
                                       [1, 0] + list(range(2, 32))), s_v)

            @blk.tensor
            def _(t):
                t.wait_ge(s_v, C.val(s_v))
                C.inc(t.matmul(pbias[:, :], onerow[0:1, :], mrow[0:1, 0:8],
                               start=True, stop=True), s_t)

            @blk.vector
            def _(v):
                v.wait_ge(s_t, C.val(s_t))
                C.inc(v.tensor_copy(sbap(biasall, 16, 0, 128, 0, [[2, 8]]),
                                    pbias[:, :]), s_v)

            @blk.tensor
            def _(t):
                t.wait_ge(s_v, C.val(s_v))
                i2 = t.matmul(pbias[:, :], onerow[0:1, :], sq[0:1, 0:8], start=True, stop=True)
                C.inc(i2, s_t)

            @blk.vector
            def _(v):
                v.wait_ge(s_t, C.val(s_t))
                C.inc(v.tensor_copy(sbap(biasall, 16, 0, 128, 1, [[2, 8]]),
                                    pbias[:, :]), s_v)

    if phase_on():
        bias_assembly()
        tap("biasall", biasall[:, :], 128, 16)

    def emit_attention(causal):
        qtile = sq[:, 2048:4096]
        pending = [None]
        with (
            nc.sbuf_tensor(_u("ktile"), [128, 4096], F32) as ktile,
            nc.sbuf_tensor(_u("vtile"), [128, 32 * 193], F32) as vtile,

            nc.psum_tensor(_u("psc0"), [128, 512], F32) as psc0,
            nc.psum_tensor(_u("psc1"), [128, 512], F32) as psc1,
            nc.psum_tensor(_u("poe"), [128, 512], F32) as poe,
            nc.psum_tensor(_u("poo"), [128, 512], F32) as poo,
            nc.psum_tensor(_u("prc"), [128, 512], F32) as prc,
        ):
            for hp in range(8):
                with nc.Block() as blk:
                    if pending[0] is not None:
                        emit_evac_outer = pending[0]
                        pending[0] = None
                        hl_, qt_, hp_ = emit_evac_outer

                        @blk.vector
                        def _(v, hl_=hl_):
                            v.wait_ge(s_x, C.val(s_x))
                            po = poe if hl_ == 0 else poo
                            dr = 64 if hl_ == 0 else 32
                            C.inc(v.reciprocal(sq[0:1, 1024:1536], po[dr:dr + 1, 0:512]), s_v)

                        @blk.tensor
                        def _(t, hl_=hl_):
                            t.wait_ge(s_v, C.val(s_v))
                            if hl_ == 0:
                                i2 = t.matmul(prc[0:64, 0:512], onerow[0:1, 0:64],
                                              sq[0:1, 1024:1536], start=True, stop=True)
                            else:
                                i2 = t.matmul(prc[64:128, 0:512], onerow[0:1, 0:64],
                                              sq[0:1, 1024:1536], start=True, stop=True)
                            C.inc(i2, s_t)

                        @blk.vector
                        def _(v, hl_=hl_, hp_=hp_, qt_=qt_):
                            v.wait_ge(s_t, C.val(s_t))
                            if hl_ == 0:
                                C.inc(v.tensor_copy(sq[0:64, 1536:2048], prc[0:64, :]), s_v)
                                C.inc(v.scalar_tensor_tensor(
                                    sbap(work, 16384, 0, 64, hp_ * 2048 + qt_ * 512, [[1, 512]]),
                                    poe[0:64, 0:512], 1.0, sq[0:64, 1536:2048],
                                    ALU.mult, ALU.mult), s_v)
                            else:
                                C.inc(v.tensor_copy(sq[64:128, 1536:2048], prc[64:128, :]), s_v)
                                C.inc(v.scalar_tensor_tensor(
                                    sbap(work, 16384, 64, 64, hp_ * 2048 + qt_ * 512, [[1, 512]]),
                                    poo[64:128, 0:512], 1.0, sq[64:128, 1536:2048],
                                    ALU.mult, ALU.mult), s_v)

                    @blk.sync
                    def _(s, hp=hp):
                        s.wait_ge(s_t, C.val(s_t))
                        s.wait_ge(s_v, C.val(s_v))
                        s.wait_ge(s_a, C.val(s_a))
                        s.wait_ge(s_st, C.val(s_st))
                        s.wait_ge(s_g, C.val(s_g))
                        if causal:
                            C.inc(s.dma_start(
                                out=ktile[:, :],
                                in_=bass.AP(kbuf, hp * 128 * T, [[T, 128], [1, T]])), s_ld, 16)
                            C.inc(s.dma_start(
                                out=sbap(vtile, 32 * 193, 0, 128, 0, [[193, 32], [1, 64]]),
                                in_=bass.AP(vbuf, hp * T * 128,
                                            [[128, 128], [128 * 128, 32], [1, 64]])), s_ld, 16)
                            C.inc(s.dma_start(
                                out=sbap(vtile, 32 * 193, 0, 128, 129, [[193, 32], [1, 64]]),
                                in_=bass.AP(vbuf, hp * T * 128 + 64,
                                            [[128, 128], [128 * 128, 32], [1, 64]])), s_ld, 16)
                        else:
                            for rk in range(2):
                                C.inc(s.dma_start(
                                    out=ktile[:, rk * TL:(rk + 1) * TL],
                                    in_=bass.AP(cout, rk * CIN_N + CIN_K + hp * 128 * TL,
                                                [[TL, 128], [1, TL]])), s_ld, 16)
                                C.inc(s.dma_start(
                                    out=sbap(vtile, 32 * 193, 0, 128, rk * 16 * 193,
                                             [[193, 16], [1, 64]]),
                                    in_=bass.AP(cout, rk * CIN_N + CIN_V + hp * TL * 128,
                                                [[128, 128], [128 * 128, 16], [1, 64]])), s_ld, 16)
                                C.inc(s.dma_start(
                                    out=sbap(vtile, 32 * 193, 0, 128, rk * 16 * 193 + 129,
                                             [[193, 16], [1, 64]]),
                                    in_=bass.AP(cout, rk * CIN_N + CIN_V + hp * TL * 128 + 64,
                                                [[128, 128], [128 * 128, 16], [1, 64]])), s_ld, 16)
                        C.inc(s.dma_start(
                            out=qtile[:, :],
                            in_=bass.AP(qbuf, hp * 128 * TL, [[TL, 128], [1, TL]])), s_ld, 16)

                    @blk.vector
                    def _(v):
                        v.wait_ge(s_ld, C.val(s_ld))
                        C.inc(v.memset(sbap(vtile, 32 * 193, 0, 128, 64, [[193, 32], [1, 1]]), 1.0), s_v)
                        C.inc(v.memset(sbap(vtile, 32 * 193, 0, 128, 65, [[193, 32], [1, 64]]), 0.0), s_v)
                        C.inc(v.memset(sbap(vtile, 32 * 193, 0, 128, 97, [[193, 32], [1, 1]]), 1.0), s_v)

                def emit_evac(pend):
                    hl, qt, hp_ = pend

                    @blk.vector
                    def _(v, hl=hl):
                        v.wait_ge(s_x, C.val(s_x))
                        po = poe if hl == 0 else poo
                        dr = 64 if hl == 0 else 32
                        C.inc(v.reciprocal(sq[0:1, 1024:1536], po[dr:dr + 1, 0:512]), s_v)

                    @blk.tensor
                    def _(t, hl=hl):
                        t.wait_ge(s_v, C.val(s_v))
                        if hl == 0:
                            i2 = t.matmul(prc[0:64, 0:512], onerow[0:1, 0:64],
                                          sq[0:1, 1024:1536], start=True, stop=True)
                        else:
                            i2 = t.matmul(prc[64:128, 0:512], onerow[0:1, 0:64],
                                          sq[0:1, 1024:1536], start=True, stop=True)
                        C.inc(i2, s_t)

                    @blk.vector
                    def _(v, hl=hl, hp_=hp_, qt=qt):
                        v.wait_ge(s_t, C.val(s_t))
                        if hl == 0:
                            C.inc(v.tensor_copy(sq[0:64, 1536:2048], prc[0:64, :]), s_v)
                            C.inc(v.scalar_tensor_tensor(
                                sbap(work, 16384, 0, 64, hp_ * 2048 + qt * 512, [[1, 512]]),
                                poe[0:64, 0:512], 1.0, sq[0:64, 1536:2048],
                                ALU.mult, ALU.mult), s_v)
                        else:
                            C.inc(v.tensor_copy(sq[64:128, 1536:2048], prc[64:128, :]), s_v)
                            C.inc(v.scalar_tensor_tensor(
                                sbap(work, 16384, 64, 64, hp_ * 2048 + qt * 512, [[1, 512]]),
                                poo[64:128, 0:512], 1.0, sq[64:128, 1536:2048],
                                ALU.mult, ALU.mult), s_v)

                for qt in range(4):
                    for hl in range(2):
                        h = 2 * hp + hl
                        if causal:
                            chunks = [(kc, 0) for kc in range(8 * qt + 2)]
                            for r in range(1, 4):
                                chunks.append((8 * qt + 2 * r, 128 * r))
                                chunks.append((8 * qt + 2 * r + 1, 128 * r))
                        else:
                            chunks = [(kc, 0) for kc in range(32)]
                        nch = len(chunks)
                        with nc.Block() as blk:
                            if pending[0] is not None:
                                emit_evac(pending[0])
                                pending[0] = None
                            t_base = C.val(s_t)
                            a_base = C.val(s_a)
                            v_base = C.val(s_v)

                            @blk.tensor
                            def _(t, hp=hp, hl=hl, qt=qt, chunks=chunks,
                                  t_base=t_base, a_base=a_base, v_base=v_base):
                                t.wait_ge(s_ld, C.val(s_ld))
                                po = poe if hl == 0 else poo
                                for i, (kc, qlo) in enumerate(chunks):
                                    psc = psc0 if (i % 2 == 0) else psc1
                                    nq = 512 - qlo
                                    if i >= 2:
                                        t.wait_ge(s_a, a_base + i - 1)
                                    else:
                                        t.wait_ge(s_a, a_base)
                                    C.inc(t.matmul(
                                        psc[0:128, 0:nq],
                                        ktile[hl * 64:hl * 64 + 64, kc * 128:(kc + 1) * 128],
                                        qtile[hl * 64:hl * 64 + 64,
                                              qt * 512 + qlo:(qt + 1) * 512],
                                        start=True, stop=True), s_t)
                                    if i >= 1:
                                        pkc, pqlo = chunks[i - 1]
                                        pnq = 512 - pqlo
                                        t.wait_ge(s_a, a_base + i)
                                        if i == 1:
                                            t.wait_ge(s_v, v_base)
                                        vsl = (vtile[:, pkc * 193:pkc * 193 + 65] if hl == 0
                                               else vtile[:, pkc * 193 + 65:pkc * 193 + 193])
                                        t.matmul(po[0:(65 if hl == 0 else 128), pqlo:512],
                                                 vsl,
                                                 sq[:, ((i - 1) % 2) * 512:
                                                    ((i - 1) % 2) * 512 + pnq],
                                                 start=(i == 1), stop=False)
                                lkc, lqlo = chunks[-1]
                                lnq = 512 - lqlo
                                t.wait_ge(s_a, a_base + nch)
                                if nch == 1:
                                    t.wait_ge(s_v, v_base)
                                vsl = (vtile[:, lkc * 193:lkc * 193 + 65] if hl == 0
                                       else vtile[:, lkc * 193 + 65:lkc * 193 + 193])
                                C.inc(t.matmul(po[0:(65 if hl == 0 else 128), lqlo:512],
                                               vsl,
                                               sq[:, ((nch - 1) % 2) * 512:
                                                  ((nch - 1) % 2) * 512 + lnq],
                                               start=(nch == 1), stop=True), s_x)

                            @blk.scalar
                            def _(a, h=h, chunks=chunks, t_base=t_base):
                                for i, (kc, qlo) in enumerate(chunks):
                                    nq = 512 - qlo
                                    psc = psc0 if (i % 2 == 0) else psc1
                                    a.wait_ge(s_t, t_base + i + 1)
                                    C.inc(a.activation(
                                        sq[:, (i % 2) * 512:(i % 2) * 512 + nq],
                                        psc[0:128, 0:nq], AF.Exp,
                                        bias=biasall[:, h:h + 1], scale=SCALE), s_a)

                            pending[0] = (hl, qt, hp)

            with nc.Block() as blk:
                emit_evac(pending[0])
                pending[0] = None

    if phase_on():
        emit_attention(causal=True)
        tap("attnO", work[:, 0:16384], 128, 16384)

    def emit_out_proj(w_hbm, with_bias, load_x):
        with (
            nc.sbuf_tensor(_u("wot"), [128, 8192], F32) as wot,
            nc.psum_tensor(_u("pw0"), [128, 512], F32) as pw0,
            nc.psum_tensor(_u("pw1"), [128, 512], F32) as pw1,
        ):
            pws = [pw0, pw1]
            with nc.Block() as blk:
                @blk.sync
                def _(s):
                    C.inc(s.dma_start(
                        out=sbap(wot, 8192, 0, 128, 0, [[1024, 8], [1, 1024]]),
                        in_=bass.AP(w_hbm, 0, [[1024, 128], [128 * 1024, 8], [1, 1024]])), s_ld, 16)
                    if load_x:
                        C.inc(s.dma_start(
                            out=sbap(xres, 8 * TL, 0, 128, 0, [[TL, 8], [1, TL]]),
                            in_=bass.AP(xmyT, 0, [[TL, 128], [128 * TL, 8], [1, TL]])), s_ld, 16)
            with nc.Block() as blk:
                v_base = C.val(s_v)
                t_base = C.val(s_t)

                @blk.tensor
                def _(t, v_base=v_base):
                    t.wait_ge(s_ld, C.val(s_ld))
                    t.wait_ge(s_v, v_base)
                    k = 0
                    for oc in range(8):
                        for qt in range(4):
                            if k >= 2:
                                t.wait_ge(s_v, v_base + k - 1)
                            for ic in range(8):
                                i2 = t.matmul(pws[k % 2][:, :],
                                              wot[:, ic * 1024 + oc * 128: ic * 1024 + oc * 128 + 128],
                                              work[:, ic * 2048 + qt * 512: ic * 2048 + qt * 512 + 512],
                                              start=(ic == 0), stop=(ic == 7))
                            C.inc(i2, s_t)
                            k += 1

                @blk.vector
                def _(v, t_base=t_base):
                    k = 0
                    for oc in range(8):
                        for qt in range(4):
                            v.wait_ge(s_t, t_base + k + 1)
                            dst = xres[:, oc * TL + qt * 512: oc * TL + qt * 512 + 512]
                            if with_bias:
                                C.inc(v.scalar_tensor_tensor(
                                    dst, pws[k % 2][:, :], bqvt[:, oc * 4 + 3:oc * 4 + 4], dst,
                                    ALU.add, ALU.add), s_v)
                            else:
                                C.inc(v.scalar_tensor_tensor(
                                    dst, pws[k % 2][:, :], 1.0, dst, ALU.mult, ALU.add), s_v)
                            k += 1

    if phase_on():
        emit_out_proj(wo_i, with_bias=False, load_x=True)
        tap("x1", xres[:, 0:16384], 128, 16384)

    def emit_ffn(cdense):
        with (
            nc.psum_tensor(_u("pstat1"), [1, 512], F32) as pstat1,
            nc.psum_tensor(_u("pstat2"), [1, 512], F32) as pstat2,
            nc.psum_tensor(_u("pbc1"), [128, 512], F32) as pbc1,
            nc.psum_tensor(_u("pbc2"), [128, 512], F32) as pbc2,
        ):
            for tt in range(4):
                with nc.Block() as blk:
                    ln_tile(blk,
                            lambda c, tt=tt: xres[:, c * TL + tt * 512: c * TL + tt * 512 + 512],
                            lambda c, tt=tt: work[:, c * TL + tt * 512: c * TL + tt * 512 + 512],
                            512, pstat1, pstat2, pbc1, pbc2)

        with (
            nc.sbuf_tensor(_u("rk8"), [8, 3 * 2048], F32) as rk8,
            nc.psum_tensor(_u("prt"), [8, 512], F32) as prt,
            nc.psum_tensor(_u("pbc"), [8, 512], F32) as pbc,
        ):
            for tt in range(4):
                with nc.Block() as blk:
                    @blk.tensor
                    def _(t, tt=tt):
                        t.wait_ge(s_v, C.val(s_v))
                        for ic in range(8):
                            i2 = t.matmul(prt[:, :], routw[:, ic * 8:(ic + 1) * 8],
                                          work[:, ic * TL + tt * 512: ic * TL + tt * 512 + 512],
                                          start=(ic == 0), stop=(ic == 7))
                        C.inc(i2, s_t)

                    @blk.vector
                    def _(v, tt=tt):
                        v.wait_ge(s_t, C.val(s_t))
                        C.inc(v.tensor_copy(cdense[:, tt * 512:(tt + 1) * 512], prt[:, :]), s_v)

            P0 = rk8[0:8, 0:2048]
            P1 = rk8[0:8, 2048:4096]
            P2 = rk8[0:8, 4096:6144]
            with nc.Block() as blk:
                @blk.vector
                def _(v):
                    v.wait_ge(s_v, C.val(s_v))
                    C.inc(v.stream_shuffle(P0, cdense[:, :],
                                           [(j + 4) % 8 if j < 8 else j for j in range(32)]), s_v)
                    C.inc(v.tensor_max(P1[0:4, :], cdense[0:4, :], P0[0:4, :]), s_v)
                    C.inc(v.stream_shuffle(P0[0:4, :], P1[0:4, :],
                                           [(j + 2) % 4 if j < 4 else j for j in range(32)]), s_v)
                    C.inc(v.tensor_max(P1[0:2, :], P1[0:2, :], P0[0:2, :]), s_v)
                    C.inc(v.stream_shuffle(P0[0:2, :], P1[0:2, :],
                                           [1 - j if j < 2 else j for j in range(32)]), s_v)
                    C.inc(v.tensor_max(P1[0:1, :], P1[0:1, :], P0[0:1, :]), s_v)

                v_base = C.val(s_v)

                @blk.tensor
                def _(t, v_base=v_base):
                    t.wait_ge(s_v, v_base)
                    for tt in range(4):
                        if tt >= 1:
                            t.wait_ge(s_v, v_base + tt)
                        C.inc(t.matmul(pbc[:, :], onerow[0:1, 0:8],
                                       P1[0:1, tt * 512:(tt + 1) * 512],
                                       start=True, stop=True), s_t)

                @blk.vector
                def _(v):
                    base = C.val(s_t) - 4
                    for tt in range(4):
                        v.wait_ge(s_t, base + tt + 1)
                        sl = slice(tt * 512, (tt + 1) * 512)
                        C.inc(v.tensor_tensor(cdense[:, sl], cdense[:, sl], pbc[:, :],
                                              ALU.subtract), s_v)
            with nc.Block() as blk:
                @blk.scalar
                def _(a):
                    a.wait_ge(s_v, C.val(s_v))
                    C.inc(a.activation(cdense[:, :], cdense[:, :], AF.Exp), s_a)

                @blk.vector
                def _(v):
                    v.wait_ge(s_a, C.val(s_a))
                    C.inc(v.stream_shuffle(P0, cdense[:, :],
                                           [(j + 4) % 8 if j < 8 else j for j in range(32)]), s_v)
                    C.inc(v.tensor_add(P1[0:4, :], cdense[0:4, :], P0[0:4, :]), s_v)
                    C.inc(v.stream_shuffle(P0[0:4, :], P1[0:4, :],
                                           [(j + 2) % 4 if j < 4 else j for j in range(32)]), s_v)
                    C.inc(v.tensor_add(P1[0:2, :], P1[0:2, :], P0[0:2, :]), s_v)
                    C.inc(v.stream_shuffle(P0[0:2, :], P1[0:2, :],
                                           [1 - j if j < 2 else j for j in range(32)]), s_v)
                    C.inc(v.tensor_add(P1[0:1, :], P1[0:1, :], P0[0:1, :]), s_v)
                    C.inc(v.reciprocal(tiny[0:1, 0:2048], P1[0:1, :]), s_v)
                    C.inc(v.memset(P2, 0.0), s_v)
                    for kk in range(1, 8):
                        C.inc(v.stream_shuffle(P0, cdense[:, :],
                                               [(j + kk) % 8 if j < 8 else j for j in range(32)]), s_v)
                        C.inc(v.tensor_tensor(P0, P0, cdense[:, :], ALU.is_gt), s_v)
                        C.inc(v.tensor_add(P2, P2, P0), s_v)
                        C.inc(v.stream_shuffle(P0, cdense[:, :],
                                               [(j + kk) % 8 if j < 8 else j for j in range(32)]), s_v)
                        C.inc(v.tensor_tensor(P0, P0, cdense[:, :], ALU.is_equal), s_v)
                        C.inc(v.tensor_scalar_mul(P0, P0, rankw[:, kk - 1:kk]), s_v)
                        C.inc(v.tensor_add(P2, P2, P0), s_v)
                    C.inc(v.tensor_single_scalar(P0, P2, 2.0, ALU.is_lt), s_v)
                    C.inc(v.tensor_tensor(cdense[:, :], cdense[:, :], P0, ALU.mult), s_v)

                v_base2 = C.val(s_v)

                @blk.tensor
                def _(t, v_base2=v_base2):
                    t.wait_ge(s_v, v_base2)
                    for tt in range(4):
                        if tt >= 1:
                            t.wait_ge(s_v, v_base2 + tt)
                        C.inc(t.matmul(pbc[:, :], onerow[0:1, 0:8],
                                       tiny[0:1, tt * 512:(tt + 1) * 512],
                                       start=True, stop=True), s_t)

                @blk.vector
                def _(v):
                    base = C.val(s_t) - 4
                    for tt in range(4):
                        v.wait_ge(s_t, base + tt + 1)
                        sl = slice(tt * 512, (tt + 1) * 512)
                        C.inc(v.tensor_tensor(cdense[:, sl], cdense[:, sl], pbc[:, :],
                                              ALU.mult), s_v)

        tap("cdense", cdense[:, :], 8, 2048)

        crowz = tiny[0:8, 0:2048]
        with (
            nc.sbuf_tensor(_u("wgu"), [128, 4096], F32) as wgu,
            nc.sbuf_tensor(_u("wdn"), [128, 2048], F32) as wdn,
            nc.sbuf_tensor(_u("gs"), [128, 512], F32) as gs,
            nc.psum_tensor(_u("pg0"), [128, 512], F32) as pg0,
            nc.psum_tensor(_u("pg1"), [128, 512], F32) as pg1,
            nc.psum_tensor(_u("pu0"), [128, 512], F32) as pu0,
            nc.psum_tensor(_u("pu1"), [128, 512], F32) as pu1,
            nc.psum_tensor(_u("pc0"), [128, 512], F32) as pc0,
            nc.psum_tensor(_u("py0"), [128, 512], F32) as py0,
            nc.psum_tensor(_u("py1"), [128, 512], F32) as py1,
        ):
            pgs = [pg0, pg1]
            pus = [pu0, pu1]
            pys = [py0, py1]
            for e in range(8):
                with nc.Block() as blk:
                    @blk.vector
                    def _(v, e=e):
                        v.wait_ge(s_v, C.val(s_v))
                        C.inc(v.stream_shuffle(crowz[:, :], cdense[:, :],
                                               [e if j < 8 else j for j in range(32)]), s_v)
                for tq in range(4):
                    with nc.Block() as blk:
                        @blk.tensor
                        def _(t, tq=tq):
                            t.wait_ge(s_v, C.val(s_v))
                            C.inc(t.matmul(pc0[:, :], onerow[0:1, :],
                                           crowz[0:1, tq * 512:(tq + 1) * 512],
                                           start=True, stop=True), s_t)
                        ld_base = C.val(s_ld)
                        a_base = C.val(s_a)
                        v_base = C.val(s_v)
                        t_base = C.val(s_t)

                        @blk.sync
                        def _(s, e=e, ld_base=ld_base, a_base=a_base, v_base=v_base):
                            for fc in range(8):
                                if fc >= 2:
                                    s.wait_ge(s_t, t_base + fc)
                                else:
                                    s.wait_ge(s_t, t_base)
                                off = (fc % 2) * 2048
                                C.inc(s.dma_start(
                                    out=sbap(wgu, 4096, 0, 128, off, [[128, 8], [1, 128]]),
                                    in_=bass.AP(gateB, (e * 64 + fc * 8) * 128 * 128,
                                                [[128, 128], [128 * 128, 8], [1, 128]])), s_ld, 16)
                                C.inc(s.dma_start(
                                    out=sbap(wgu, 4096, 0, 128, off + 1024, [[128, 8], [1, 128]]),
                                    in_=bass.AP(upB, (e * 64 + fc * 8) * 128 * 128,
                                                [[128, 128], [128 * 128, 8], [1, 128]])), s_ld, 16)

                        @blk.tensor
                        def _(t, e=e, tq=tq, ld_base=ld_base, a_base=a_base, v_base=v_base):
                            zoff = tq * 512
                            for fc in range(8):
                                off = (fc % 2) * 2048
                                t.wait_ge(s_ld, ld_base + 32 * (fc + 1))
                                if fc >= 2:
                                    t.wait_ge(s_a, a_base + fc - 1)
                                    t.wait_ge(s_v, v_base + 2 * (fc - 1))
                                for ic in range(8):
                                    t.matmul(pgs[fc % 2][:, :],
                                             wgu[:, off + ic * 128: off + (ic + 1) * 128],
                                             work[:, ic * TL + zoff: ic * TL + zoff + 512],
                                             start=(ic == 0), stop=(ic == 7))
                                for ic in range(8):
                                    i2 = t.matmul(pus[fc % 2][:, :],
                                                  wgu[:, off + 1024 + ic * 128:
                                                      off + 1024 + (ic + 1) * 128],
                                                  work[:, ic * TL + zoff: ic * TL + zoff + 512],
                                                  start=(ic == 0), stop=(ic == 7))
                                C.inc(i2, s_t)

                        @blk.scalar
                        def _(a, t_base=t_base, v_base=v_base):
                            for fc in range(8):
                                a.wait_ge(s_t, t_base + fc + 1)
                                if fc >= 1:
                                    a.wait_ge(s_v, v_base + 2 * fc)
                                C.inc(a.activation(gs[:, :],
                                                   pgs[fc % 2][:, :], AF.Silu), s_a)

                        @blk.vector
                        def _(v, a_base=a_base):
                            for fc in range(8):
                                v.wait_ge(s_a, a_base + fc + 1)
                                gsl = gs[:, :]
                                C.inc(v.tensor_tensor(gsl, gsl, pus[fc % 2][:, :],
                                                      ALU.mult), s_v)
                                C.inc(v.tensor_tensor(sq[:, fc * 512:(fc + 1) * 512],
                                                      gsl, pc0[:, :], ALU.mult), s_v)

                        ld2 = C.val(s_ld)
                        t2 = C.val(s_t)

                        @blk.sync
                        def _(s, e=e, ld2=ld2, t2=t2):
                            for oc in range(8):
                                if oc >= 2:
                                    s.wait_ge(s_t, t2 + oc - 1)
                                else:
                                    s.wait_ge(s_t, t2)
                                C.inc(s.dma_start(
                                    out=sbap(wdn, 2048, 0, 128, (oc % 2) * 1024,
                                             [[128, 8], [1, 128]]),
                                    in_=bass.AP(downB, (e * 64 + oc * 8) * 128 * 128,
                                                [[128, 128], [128 * 128, 8], [1, 128]])),
                                    s_ld, 16)

                        @blk.tensor
                        def _(t, v_base=v_base, ld2=ld2):
                            t.wait_ge(s_v, v_base + 16)
                            for oc in range(8):
                                t.wait_ge(s_ld, ld2 + 16 * (oc + 1))
                                if oc >= 2:
                                    t.wait_ge(s_v, v_base + 16 + oc - 1)
                                for fc in range(8):
                                    i2 = t.matmul(pys[oc % 2][:, :],
                                                  wdn[:, (oc % 2) * 1024 + fc * 128:
                                                      (oc % 2) * 1024 + fc * 128 + 128],
                                                  sq[:, fc * 512:(fc + 1) * 512],
                                                  start=(fc == 0), stop=(fc == 7))
                                C.inc(i2, s_t)

                        @blk.vector
                        def _(v, tq=tq):
                            base = C.val(s_t) - 8
                            for oc in range(8):
                                v.wait_ge(s_t, base + oc + 1)
                                dst = xres[:, oc * TL + tq * 512: oc * TL + tq * 512 + 512]
                                C.inc(v.scalar_tensor_tensor(dst, pys[oc % 2][:, :], 1.0, dst,
                                                             ALU.mult, ALU.add), s_v)

        with (
            nc.sbuf_tensor(_u("wgu"), [128, 4096], F32) as wgu,
            nc.sbuf_tensor(_u("wdn"), [128, 2048], F32) as wdn,
            nc.sbuf_tensor(_u("gs"), [128, 512], F32) as gs,
            nc.psum_tensor(_u("pg0"), [128, 256], F32) as pg0,
            nc.psum_tensor(_u("pg1"), [128, 256], F32) as pg1,
            nc.psum_tensor(_u("pu0"), [128, 256], F32) as pu0,
            nc.psum_tensor(_u("pu1"), [128, 256], F32) as pu1,
            nc.psum_tensor(_u("py0"), [128, 256], F32) as py0,
            nc.psum_tensor(_u("py1"), [128, 256], F32) as py1,
        ):
            pgs = [pg0, pg1]
            pus = [pu0, pu1]
            pys = [py0, py1]
            for dh in range(2):
                for tq in range(8):
                    with nc.Block() as blk:
                        ld_base = C.val(s_ld)
                        a_base = C.val(s_a)
                        v_base = C.val(s_v)
                        t_base = C.val(s_t)

                        @blk.sync
                        def _(s, dh=dh, t_base=t_base):
                            for i8 in range(8):
                                fc = dh * 8 + i8
                                s.wait_ge(s_t, t_base + max(i8 - 1, 0))
                                off = (i8 % 2) * 2048
                                C.inc(s.dma_start(
                                    out=sbap(wgu, 4096, 0, 128, off, [[128, 8], [1, 128]]),
                                    in_=bass.AP(sgateB, fc * 8 * 128 * 128,
                                                [[128, 128], [128 * 128, 8], [1, 128]])), s_ld, 16)
                                C.inc(s.dma_start(
                                    out=sbap(wgu, 4096, 0, 128, off + 1024, [[128, 8], [1, 128]]),
                                    in_=bass.AP(supB, fc * 8 * 128 * 128,
                                                [[128, 128], [128 * 128, 8], [1, 128]])), s_ld, 16)

                        @blk.tensor
                        def _(t, tq=tq, ld_base=ld_base, a_base=a_base, v_base=v_base):
                            zoff = tq * 256
                            for i8 in range(8):
                                off = (i8 % 2) * 2048
                                t.wait_ge(s_ld, ld_base + 32 * (i8 + 1))
                                if i8 >= 2:
                                    t.wait_ge(s_a, a_base + i8 - 1)
                                    t.wait_ge(s_v, v_base + 2 * (i8 - 1))
                                for ic in range(8):
                                    t.matmul(pgs[i8 % 2][:, :],
                                             wgu[:, off + ic * 128: off + (ic + 1) * 128],
                                             work[:, ic * TL + zoff: ic * TL + zoff + 256],
                                             start=(ic == 0), stop=(ic == 7))
                                for ic in range(8):
                                    i2 = t.matmul(pus[i8 % 2][:, :],
                                                  wgu[:, off + 1024 + ic * 128:
                                                      off + 1024 + (ic + 1) * 128],
                                                  work[:, ic * TL + zoff: ic * TL + zoff + 256],
                                                  start=(ic == 0), stop=(ic == 7))
                                C.inc(i2, s_t)

                        @blk.scalar
                        def _(a, t_base=t_base):
                            for i8 in range(8):
                                a.wait_ge(s_t, t_base + i8 + 1)
                                C.inc(a.activation(gs[:, (i8 % 2) * 256:(i8 % 2) * 256 + 256],
                                                   pgs[i8 % 2][:, :], AF.Silu), s_a)

                        @blk.vector
                        def _(v, a_base=a_base):
                            for i8 in range(8):
                                v.wait_ge(s_a, a_base + i8 + 1)
                                gsl = gs[:, (i8 % 2) * 256:(i8 % 2) * 256 + 256]
                                C.inc(v.tensor_tensor(gsl, gsl, pus[i8 % 2][:, :],
                                                      ALU.mult), s_v)
                                C.inc(v.tensor_copy(sq[:, i8 * 256:(i8 + 1) * 256], gsl), s_v)

                        ld2 = C.val(s_ld)
                        t2 = C.val(s_t)

                        @blk.sync
                        def _(s, dh=dh, ld2=ld2, t2=t2):
                            for oc in range(8):
                                if oc >= 2:
                                    s.wait_ge(s_t, t2 + oc - 1)
                                else:
                                    s.wait_ge(s_t, t2)
                                C.inc(s.dma_start(
                                    out=sbap(wdn, 2048, 0, 128, (oc % 2) * 1024,
                                             [[128, 8], [1, 128]]),
                                    in_=bass.AP(sdownB, (dh * 64 + oc * 8) * 128 * 128,
                                                [[128, 128], [128 * 128, 8], [1, 128]])),
                                    s_ld, 16)

                        @blk.tensor
                        def _(t, v_base=v_base, ld2=ld2):
                            t.wait_ge(s_v, v_base + 16)
                            for oc in range(8):
                                t.wait_ge(s_ld, ld2 + 16 * (oc + 1))
                                if oc >= 2:
                                    t.wait_ge(s_v, v_base + 16 + oc - 1)
                                for i8 in range(8):
                                    i2 = t.matmul(pys[oc % 2][:, :],
                                                  wdn[:, (oc % 2) * 1024 + i8 * 128:
                                                      (oc % 2) * 1024 + i8 * 128 + 128],
                                                  sq[:, i8 * 256:(i8 + 1) * 256],
                                                  start=(i8 == 0), stop=(i8 == 7))
                                C.inc(i2, s_t)

                        @blk.vector
                        def _(v, tq=tq):
                            base = C.val(s_t) - 8
                            for oc in range(8):
                                v.wait_ge(s_t, base + oc + 1)
                                dst = xres[:, oc * TL + tq * 256: oc * TL + tq * 256 + 256]
                                C.inc(v.scalar_tensor_tensor(dst, pys[oc % 2][:, :], 1.0, dst,
                                                             ALU.mult, ALU.add), s_v)

    if phase_on():
        with nc.sbuf_tensor(_u("cdense1"), [8, 2048], F32) as cdense:
            emit_ffn(cdense)
        tap("x2", xres[:, 0:16384], 128, 16384)

    if phase_on():
        with (
            nc.psum_tensor(_u("pstat1"), [1, 512], F32) as pstat1,
            nc.psum_tensor(_u("pstat2"), [1, 512], F32) as pstat2,
            nc.psum_tensor(_u("pbc1"), [128, 512], F32) as pbc1,
            nc.psum_tensor(_u("pbc2"), [128, 512], F32) as pbc2,
        ):
            for tt in range(4):
                with nc.Block() as blk:
                    ln_tile(blk,
                            lambda c, tt=tt: xres[:, c * TL + tt * 512: c * TL + tt * 512 + 512],
                            lambda c, tt=tt: work[:, c * TL + tt * 512: c * TL + tt * 512 + 512],
                            512, pstat1, pstat2, pbc1, pbc2)

        with (
            nc.sbuf_tensor(_u("wpt"), [128, 8192], F32) as wpt,
            nc.sbuf_tensor(_u("praw"), [128, 4096], F32) as praw,

            nc.psum_tensor(_u("pq0"), [128, 512], F32) as pq0,
            nc.psum_tensor(_u("pstat1"), [1, 512], F32) as pstat1,
            nc.psum_tensor(_u("pstat2"), [1, 512], F32) as pstat2,
            nc.psum_tensor(_u("pbc1"), [128, 512], F32) as pbc1,
            nc.psum_tensor(_u("pn"), [2, 512], F32) as pn,
        ):
            kv1 = sq[:, 0:512]
            kv2 = sq[:, 512:1024]
            bvrow = tiny[0:1, 0:1024]
            bvb = sq[:, 1024:2048]
            for which in ("q", "k"):
                w_hbm = WqT_i if which == "q" else WkT_i
                bcol = 0 if which == "q" else 1
                wcol = 0 if which == "q" else 1
                mcol = 8 if which == "q" else 0
                with nc.Block() as blk:
                    @blk.sync
                    def _(s, w_hbm=w_hbm):
                        s.wait_ge(s_t, C.val(s_t))
                        C.inc(s.dma_start(
                            out=sbap(wpt, 8192, 0, 128, 0, [[1024, 8], [1, 1024]]),
                            in_=bass.AP(w_hbm, 0, [[1024, 128], [128 * 1024, 8], [1, 1024]])),
                            s_ld, 16)

                    @blk.vector
                    def _(v, mcol=mcol):
                        v.wait_ge(s_v, C.val(s_v))
                        C.inc(v.memset(hmax[0:2, mcol:mcol + 8], 0.0), s_v)

                for tt in range(4):
                    with nc.Block() as blk:
                        for oc in range(8):
                            @blk.tensor
                            def _(t, oc=oc, tt=tt):
                                t.wait_ge(s_ld, C.val(s_ld))
                                t.wait_ge(s_a, C.val(s_a))
                                for ic in range(8):
                                    i2 = t.matmul(pq0[:, :],
                                                  wpt[:, ic * 1024 + oc * 128:
                                                      ic * 1024 + oc * 128 + 128],
                                                  work[:, ic * TL + tt * 512:
                                                       ic * TL + tt * 512 + 512],
                                                  start=(ic == 0), stop=(ic == 7))
                                C.inc(i2, s_t)

                            @blk.scalar
                            def _(a, oc=oc, bcol=bcol):
                                a.wait_ge(s_t, C.val(s_t))
                                C.inc(a.add(praw[:, oc * 512:(oc + 1) * 512], pq0[:, :],
                                            bqvt[:, oc * 4 + bcol:oc * 4 + bcol + 1]), s_a)

                        @blk.scalar
                        def _(a):
                            a.wait_ge(s_a, C.val(s_a))
                            C.inc(a.activation(sq[:, 0:4096], praw[:, :], AF.Square), s_a)

                        @blk.tensor
                        def _(t):
                            t.wait_ge(s_a, C.val(s_a))
                            for oc in range(8):
                                i2 = t.matmul(pstat1[0:1, 0:512], onecol[:, :],
                                              sq[:, oc * 512:(oc + 1) * 512],
                                              start=(oc == 0), stop=(oc == 7))
                            C.inc(i2, s_t)

                        @blk.vector
                        def _(v):
                            v.wait_ge(s_t, C.val(s_t))
                            var = tiny[0:1, 512:1024]
                            C.inc(v.tensor_scalar(var, pstat1[0:1, 0:512], 1.0 / 1024.0,
                                                  RMS_EPS, ALU.mult, ALU.add), s_v)
                            C.inc(v.reciprocal(tiny[0:1, 1024:1536], var), s_v)

                        @blk.scalar
                        def _(a):
                            a.wait_ge(s_v, C.val(s_v))
                            C.inc(a.activation(tiny[0:1, 1536:2048], tiny[0:1, 1024:1536],
                                               AF.Sqrt), s_a)

                        @blk.vector
                        def _(v):
                            v.wait_ge(s_a, C.val(s_a))
                            r = tiny[0:1, 1536:2048]
                            var = tiny[0:1, 512:1024]
                            tmp = tiny[0:1, 1024:1536]
                            for _i in range(2):
                                C.inc(v.tensor_tensor(tmp, r, r, ALU.mult), s_v)
                                C.inc(v.tensor_tensor(tmp, var, tmp, ALU.mult), s_v)
                                C.inc(v.tensor_scalar(tmp, tmp, -0.5, 1.5,
                                                      ALU.mult, ALU.add), s_v)
                                C.inc(v.tensor_tensor(r, r, tmp, ALU.mult), s_v)

                        @blk.tensor
                        def _(t):
                            t.wait_ge(s_v, C.val(s_v))
                            C.inc(t.matmul(pbc1[:, 0:512], onerow[0:1, :],
                                           tiny[0:1, 1536:2048],
                                           start=True, stop=True), s_t)

                        for oc in range(8):
                            @blk.vector
                            def _(v, oc=oc, wcol=wcol):
                                v.wait_ge(s_t, C.val(s_t))
                                v.wait_ge(s_st, C.val(s_st))
                                C.inc(v.tensor_tensor(kv1,
                                                      praw[:, oc * 512:(oc + 1) * 512],
                                                      pbc1[:, 0:512], ALU.mult), s_v)
                                C.inc(v.tensor_scalar_mul(kv1, kv1,
                                                          nqkt[:, oc * 2 + wcol:oc * 2 + wcol + 1]),
                                      s_v)

                            @blk.scalar
                            def _(a):
                                a.wait_ge(s_v, C.val(s_v))
                                C.inc(a.activation(kv2, kv1, AF.Square), s_a)

                            @blk.tensor
                            def _(t):
                                t.wait_ge(s_a, C.val(s_a))
                                C.inc(t.matmul(pn[:, :], bones[:, :], kv2,
                                               start=True, stop=True), s_t)

                            @blk.vector
                            def _(v, oc=oc, mcol=mcol):
                                v.wait_ge(s_t, C.val(s_t))
                                C.inc(v.reduce_max(hmax[0:2, 16:17], pn[0:2, :], axis=AX.X), s_v)
                                C.inc(v.tensor_max(hmax[0:2, mcol + oc:mcol + oc + 1],
                                                   hmax[0:2, mcol + oc:mcol + oc + 1],
                                                   hmax[0:2, 16:17]), s_v)

                            @blk.gpsimd
                            def _(g, oc=oc, tt=tt, which=which):
                                g.wait_ge(s_v, C.val(s_v))
                                if which == "q":
                                    C.inc(g.dma_start(
                                        out=bass.AP(qbuf, oc * 128 * TL + tt * 512,
                                                    [[TL, 128], [1, 512]]),
                                        in_=kv1), s_st, 16)
                                else:
                                    C.inc(g.dma_start(
                                        out=bass.AP(cin, CIN_K + oc * 128 * TL + tt * 512,
                                                    [[TL, 128], [1, 512]]),
                                        in_=kv1), s_st, 16)

            with nc.Block() as blk:
                @blk.gpsimd
                def _(g):
                    g.wait_ge(s_v, C.val(s_v))
                    C.inc(g.dma_start(out=bass.AP(cin, CIN_M, [[8, 2], [1, 8]]),
                                      in_=hmax[0:2, 0:8]), s_st, 16)

            with nc.Block() as blk:
                @blk.sync
                def _(s):
                    s.wait_ge(s_t, C.val(s_t))
                    s.wait_ge(s_v, C.val(s_v))
                    s.wait_ge(s_a, C.val(s_a))
                    C.inc(s.dma_start(out=bvrow, in_=bvrow_i.ap()), s_ld, 16)
                    C.inc(s.dma_start(
                        out=sbap(wpt, 8192, 0, 128, 0, [[1024, 8], [1, 1024]]),
                        in_=bass.AP(WvT_i, 0, [[1024, 128], [128 * 1024, 8], [1, 1024]])),
                        s_ld, 16)


            with nc.Block() as blk:
                @blk.tensor
                def _(t):
                    t.wait_ge(s_v, C.val(s_v))
                    C.inc(t.matmul(pbc1[:, 0:512], onerow[0:1, :], bvrow[0:1, 0:512],
                                   start=True, stop=True), s_t)

                @blk.vector
                def _(v):
                    v.wait_ge(s_t, C.val(s_t))
                    C.inc(v.tensor_copy(bvb[:, 0:512], pbc1[:, 0:512]), s_v)

                @blk.tensor
                def _(t):
                    t.wait_ge(s_v, C.val(s_v))
                    C.inc(t.matmul(pbc1[:, 0:512], onerow[0:1, :], bvrow[0:1, 512:1024],
                                   start=True, stop=True), s_t)

                @blk.vector
                def _(v):
                    v.wait_ge(s_t, C.val(s_t))
                    C.inc(v.tensor_copy(bvb[:, 512:1024], pbc1[:, 0:512]), s_v)

            for tt in range(16):
                for nt in range(2):
                    with nc.Block() as blk:
                        @blk.tensor
                        def _(t, tt=tt, nt=nt):
                            t.wait_ge(s_ld, C.val(s_ld))
                            t.wait_ge(s_st, C.val(s_st))
                            for ic in range(8):
                                i2 = t.matmul(pq0[:, :],
                                              work[:, ic * TL + tt * 128: ic * TL + tt * 128 + 128],
                                              wpt[:, ic * 1024 + nt * 512:
                                                  ic * 1024 + nt * 512 + 512],
                                              start=(ic == 0), stop=(ic == 7))
                            C.inc(i2, s_t)

                        @blk.vector
                        def _(v, nt=nt):
                            v.wait_ge(s_t, C.val(s_t))
                            C.inc(v.tensor_tensor(kv1, pq0[:, :],
                                                  bvb[:, nt * 512:(nt + 1) * 512], ALU.add), s_v)

                        @blk.gpsimd
                        def _(g, tt=tt, nt=nt):
                            g.wait_ge(s_v, C.val(s_v))
                            for i in range(4):
                                hp = nt * 4 + i
                                C.inc(g.dma_start(
                                    out=bass.AP(cin, CIN_V + hp * TL * 128 + tt * 128 * 128,
                                                [[128, 128], [1, 128]]),
                                    in_=kv1[:, i * 128:(i + 1) * 128]), s_st, 16)

        with nc.Block() as blk:
            @blk.gpsimd
            def _(g):
                g.wait_ge(s_st, C.val(s_st))
                C.inc(g.collective_compute(
                    "AllGather", ALU.bypass,
                    replica_groups=[[0, 1], [2, 3], [4, 5], [6, 7]],
                    ins=[cin.ap().opt()], outs=[cout.ap().opt()]), s_g)
                g.wait_ge(s_g, C.val(s_g))

            @blk.sync
            def _(s):
                s.wait_ge(s_g, C.val(s_g))
                C.inc(s.dma_start(out=hmax[0:2, 0:8],
                                  in_=bass.AP(cout, CIN_M, [[8, 2], [1, 8]])), s_ld, 16)
                C.inc(s.dma_start(out=hmax[0:2, 16 - 16:2 * 8 - 8] if False else sq[0:2, 0:8],
                                  in_=bass.AP(cout, CIN_N + CIN_M, [[8, 2], [1, 8]])), s_ld, 16)

            @blk.vector
            def _(v):
                v.wait_ge(s_ld, C.val(s_ld))
                C.inc(v.tensor_max(hmax[0:2, 0:8], hmax[0:2, 0:8], sq[0:2, 0:8]), s_v)

        bias_assembly()
        emit_attention(causal=False)
        tap("attnO2", work[:, 0:16384], 128, 16384)
        emit_out_proj(WoT_i, with_bias=True, load_x=False)
        tap("x3", xres[:, 0:16384], 128, 16384)

    if phase_on():
        with nc.sbuf_tensor(_u("cdense2"), [8, 2048], F32) as cdense:
            emit_ffn(cdense)
        tap("x4", xres[:, 0:16384], 128, 16384)

    if phase_on():
        with nc.Block() as blk:
            @blk.gpsimd
            def _(g):
                g.wait_ge(s_v, C.val(s_v))
                g.wait_ge(s_t, C.val(s_t))
                C.inc(g.dma_start(
                    out=bass.AP(outT, 0, [[TL, 128], [128 * TL, 8], [1, TL]]),
                    in_=sbap(xres, 8 * TL, 0, 128, 0, [[TL, 8], [1, TL]])), s_st, 16)
                g.wait_ge(s_st, C.val(s_st))

    es.close()
    return nc


def _rope_tables():
    inv = 1.0 / (10000.0 ** (np.arange(0, 64, 2, dtype=np.float32) / 64))
    ang = np.outer(np.arange(T, dtype=np.float32), inv)
    cos, sin = np.cos(ang).astype(np.float32), np.sin(ang).astype(np.float32)
    cdup = np.zeros((128, T), np.float32)
    sdup = np.zeros((128, T), np.float32)
    for r in range(128):
        i = (r % 64) // 2
        cdup[r] = cos[:, i]
        sdup[r] = (sin[:, i] if (r % 2 == 1) else -sin[:, i])
    return cdup, sdup


def _token_idx(h):
    j = np.arange(TL)
    return (j // 128) * 256 + h * 128 + (j % 128)


_BUILT = [None]


def kernel(**inputs):
    x0 = np.asarray(inputs["L0_attn_a_x"])
    cdup, sdup = _rope_tables()
    bones = np.zeros((128, 2), np.float32)
    bones[0:64, 0] = 1.0
    bones[64:128, 1] = 1.0
    rankwm = np.zeros((8, 7), np.float32)
    for e in range(8):
        for kk in range(1, 8):
            rankwm[e, kk - 1] = 1.0 if (e + kk) % 8 < e else 0.0

    gateBm = np.ascontiguousarray(
        np.asarray(inputs["L0_ffn_gate_w"]).reshape(8, 8, 128, 8, 128).transpose(0, 3, 1, 2, 4))
    upBm = np.ascontiguousarray(
        np.asarray(inputs["L0_ffn_up_w"]).reshape(8, 8, 128, 8, 128).transpose(0, 3, 1, 2, 4))
    downBm = np.ascontiguousarray(
        np.asarray(inputs["L0_ffn_down_w"]).reshape(8, 8, 128, 8, 128).transpose(0, 3, 1, 2, 4))
    sgateBm = np.ascontiguousarray(
        np.asarray(inputs["L0_ffn_shared_gate_w"]).reshape(8, 128, 16, 128).transpose(2, 0, 1, 3))
    supBm = np.ascontiguousarray(
        np.asarray(inputs["L0_ffn_shared_up_w"]).reshape(8, 128, 16, 128).transpose(2, 0, 1, 3))
    sdownBm = np.ascontiguousarray(
        np.asarray(inputs["L0_ffn_shared_down_w"]).reshape(2, 8, 128, 8, 128).transpose(0, 3, 1, 2, 4))
    bqvm = np.stack([np.asarray(inputs["L0_attn_b_bq"]), np.asarray(inputs["L0_attn_b_bk"]),
                     np.asarray(inputs["L0_attn_b_bv"]), np.asarray(inputs["L0_attn_b_bo"])],
                    axis=1).astype(np.float32)
    nqkm = np.stack([np.asarray(inputs["L0_attn_b_norm_q_w"]),
                     np.asarray(inputs["L0_attn_b_norm_k_w"])], axis=1).astype(np.float32)

    shared = {
        "wq": np.asarray(inputs["L0_attn_a_wq"]),
        "w_down": np.asarray(inputs["L0_attn_a_w_down"]),
        "wk_up": np.asarray(inputs["L0_attn_a_wk_up"]),
        "wv_up": np.asarray(inputs["L0_attn_a_wv_up"]),
        "wo": np.asarray(inputs["L0_attn_a_wo"]),
        "WqT": np.ascontiguousarray(np.asarray(inputs["L0_attn_b_Wq"]).T),
        "WkT": np.ascontiguousarray(np.asarray(inputs["L0_attn_b_Wk"]).T),
        "WvT": np.ascontiguousarray(np.asarray(inputs["L0_attn_b_Wv"]).T),
        "WoT": np.ascontiguousarray(np.asarray(inputs["L0_attn_b_Wo"]).T),
        "bqv": bqvm, "nqk": nqkm,
        "bvrow_in": np.ascontiguousarray(np.asarray(inputs["L0_attn_b_bv"]).reshape(1, 1024)),
        "router_w": np.asarray(inputs["L0_ffn_router_w"]),
        "gateB": gateBm, "upB": upBm, "downB": downBm,
        "sgateB": sgateBm, "supB": supBm, "sdownB": sdownBm,
        "cdup_in": cdup, "sdup_in": sdup, "bones": bones, "rankw": rankwm,
    }

    in_maps = []
    idxs = []
    for c in range(8):
        b, h = c // 2, c % 2
        idx = _token_idx(h)
        idxs.append(idx)
        xseqTm = np.ascontiguousarray(x0[b].T)
        m = dict(shared)
        m["xseqT"] = xseqTm
        m["xmyT"] = np.ascontiguousarray(xseqTm[:, idx])
        m["cdupQ"] = np.ascontiguousarray(cdup[:, idx])
        m["sdupQ"] = np.ascontiguousarray(sdup[:, idx])
        in_maps.append(m)

    if _BUILT[0] is None:
        _BUILT[0] = build()
    nc = _BUILT[0]
    res = run_bass_kernel_spmd(nc, in_maps, core_ids=list(range(8)))

    out = np.empty((B, T, E), np.float32)
    for c in range(8):
        b, h = c // 2, c % 2
        out[b, idxs[c], :] = res.results[c]["outT"].T
    return out
